# version 06
# speedup vs baseline: 1.2026x; 1.1478x over previous
import contextlib

import concourse.bass as bass
import concourse.mybir as mybir

F32 = mybir.dt.float32
BF16 = mybir.dt.bfloat16
AF = mybir.ActivationFunctionType
ALU = mybir.AluOpType
AX = mybir.AxisListType

ENGINES = ("tensor", "vector", "scalar", "gpsimd", "sync")

D = 1024
KT = 8
T = 2048
TN = 512
NB = T // TN



class _Op:
    __slots__ = ("key", "engine", "fn", "waits", "inc", "cum", "tag", "carry",
                 "in_loop", "lwaits", "seq", "rlist", "wlist", "force")

    def __init__(self, key, engine, fn, inc, tag):
        self.key = key
        self.engine = engine
        self.fn = fn
        self.waits = {}
        self.lwaits = []
        self.inc = inc
        self.cum = 0
        self.tag = tag
        self.carry = 0
        self.in_loop = None
        self.seq = 0
        self.rlist = ()
        self.wlist = ()
        self.force = False


class _Loop:
    def __init__(self, trip):
        self.trip = trip
        self.base = {}
        self.P = {}
        self.pre_waits = {e: {} for e in ENGINES}
        self.body = {e: [] for e in ENGINES}
        self.seq = 0
        self.pad = {}


class EB:
    def __init__(self, nc, stack, self_wait=True):
        self.nc = nc
        self.stack = stack
        self.prefix = ""
        self.defer = False
        self.self_wait = self_wait
        self.sems = {}
        self.cnt = {}
        for e in ENGINES:
            for k in ("c", "d"):
                key = f"{e}:{k}"
                self.sems[key] = stack.enter_context(nc.semaphore(f"S{e}{k}"))
                self.cnt[key] = 0
        self.items = {e: [] for e in ENGINES}
        self.segs = {k: [(0, s)] for k, s in self.sems.items()}
        self.writers = {}
        self.readers = {}
        self.cur_loop = None
        self.finished = False
        self.n_ops = 0

    def sbuf(self, name, shape, dtype):
        return self.stack.enter_context(
            self.nc.sbuf_tensor(self.prefix + name, shape, dtype))

    def psum(self, name, shape, dtype=F32):
        return self.stack.enter_context(self.nc.psum_tensor(name, shape, dtype))

    @staticmethod
    def _rk(r):
        if isinstance(r, tuple):
            return tuple(EB._rk(x) for x in r)
        if isinstance(r, (str, int, float, bytes, frozenset)):
            return r
        return id(r)

    def _suppress(self, my_key, dep_key):
        if my_key != dep_key:
            return False
        eng, kind = my_key.split(":")
        if kind == "d":
            return True
        if eng == "tensor":
            return True
        return not self.self_wait

    def _dep(self, op, my_key, entry):
        if entry is None:
            return
        if entry[0] == "abs":
            _, dep_key, c2 = entry
            if self._suppress(my_key, dep_key):
                return
            if self.cur_loop is not None:
                pw = self.cur_loop.pre_waits[my_key.split(":")[0]]
                if pw.get(dep_key, 0) < c2:
                    pw[dep_key] = c2
            else:
                if op.waits.get(dep_key, 0) < c2:
                    op.waits[dep_key] = c2
        else:
            _, lp, dep_key, w2 = entry
            assert lp is self.cur_loop, "stale loop dep; loop must be closed"
            if self._suppress(my_key, dep_key):
                return
            op.lwaits.append((dep_key, "iter", w2))

    def op(self, engine, fn, reads=(), writes=(), dma=False, tag=""):
        assert not self.finished
        if fn.__closure__:
            for name, cell in zip(fn.__code__.co_freevars, fn.__closure__):
                try:
                    v = cell.cell_contents
                except ValueError:
                    continue
                assert not isinstance(v, (int, float, slice)), (
                    f"lambda captures loop-variable-like '{name}'={v!r}; "
                    f"bind it as a default argument")
        my_key = f"{engine}:{'d' if dma else 'c'}"
        reads = [self._rk(r) for r in reads]
        writes = [self._rk(w) for w in writes]
        inc = 16 if dma else 1
        o = _Op(my_key, engine, fn, inc, tag)
        for r in reads:
            self._dep(o, my_key, self.writers.get(r))
        for w in writes:
            self._dep(o, my_key, self.writers.get(w))
            for entry in self.readers.get(w, {}).values():
                self._dep(o, my_key, entry)
        self.cnt[my_key] += inc
        o.cum = self.cnt[my_key]
        o.rlist = tuple(reads)
        o.wlist = tuple(writes)
        lp = self.cur_loop
        if lp is None:
            went = ("abs", my_key, o.cum)
        else:
            went = ("loop", lp, my_key, o.cum)
            o.in_loop = lp
            o.seq = lp.seq
            lp.seq += 1
        for r in reads:
            self.readers.setdefault(r, {})[(my_key,)] = went
        for w in writes:
            self.writers[w] = went
            self.readers[w] = {}
        if lp is None:
            self.items[engine].append(o)
        else:
            lp.body[engine].append(o)
        self.n_ops += 1
        return o

    def dma(self, queue, out, in_, reads=(), writes=(), tag="", **kw):
        return self.op(queue, lambda e, i: e.dma_start(out=out, in_=in_, **kw),
                       reads=reads, writes=writes, dma=True, tag=tag)

    def barrier(self, engine="sync", tag="barrier"):
        if self.defer:
            return None
        assert self.cur_loop is None
        my_key = f"{engine}:c"
        o = _Op(my_key, engine, lambda e, i: e.nop(), 1, tag)
        o.force = True
        for k, v in self.cnt.items():
            if v > 0 and k != my_key and not self._suppress(my_key, k):
                o.waits[k] = v
        self.cnt[my_key] += 1
        o.cum = self.cnt[my_key]
        self.items[engine].append(o)
        self.n_ops += 1
        return o

    def reset_streams(self, main_stack, threshold=16000):
        assert self.cur_loop is None
        self.writers.clear()
        self.readers.clear()
        for key in list(self.cnt):
            if self.cnt[key] - self.segs[key][-1][0] > threshold:
                eng, kind = key.split(":")
                self._nsem = getattr(self, "_nsem", 0) + 1
                s = main_stack.enter_context(
                    self.nc.semaphore(f"S{eng}{kind}r{self._nsem}"))
                self.segs[key].append((self.cnt[key], s))

    def full_barrier(self, tag="fence"):
        assert self.cur_loop is None
        snap = dict(self.cnt)
        for e in ENGINES:
            my_key = f"{e}:c"
            o = _Op(my_key, e, lambda en, i: en.nop(), 1, tag)
            o.force = True
            for k, v in snap.items():
                if v > 0 and k != my_key and not self._suppress(my_key, k):
                    o.waits[k] = v
            self.cnt[my_key] += 1
            o.cum = self.cnt[my_key]
            self.items[e].append(o)
            self.n_ops += 1

    @contextlib.contextmanager
    def loop(self, trip):
        assert self.cur_loop is None
        lp = _Loop(trip)
        lp.base = dict(self.cnt)
        for k in self.cnt:
            self.cnt[k] = 0
        self.cur_loop = lp
        yield lp
        self.cur_loop = None
        lp.P = {k: v for k, v in self.cnt.items() if v > 0}

        all_body = [o for e in ENGINES for o in lp.body[e]]
        body_writers = {}
        body_readers = {}
        for o in all_body:
            for r in o.rlist:
                body_readers.setdefault(r, []).append(o)
            for w in o.wlist:
                body_writers.setdefault(w, []).append(o)

        def add_prev(consumer, producer):
            if consumer is producer:
                return
            if self._suppress(consumer.key, producer.key):
                return
            consumer.lwaits.append((producer.key, "prev", producer.cum))

        for R, ws in body_writers.items():
            W = max(ws, key=lambda o: o.seq)
            for o in body_readers.get(R, ()):
                if o.seq < W.seq:
                    add_prev(o, W)
                elif o is not W:
                    add_prev(W, o)
            for x in ws:
                if x is not W:
                    add_prev(x, W)
        for o in all_body:
            best = {}
            for (k2, mode, w2) in o.lwaits:
                kk = (k2, mode)
                if best.get(kk, -1) < w2:
                    best[kk] = w2
            o.lwaits = [(k2, mode, w2) for (k2, mode), w2 in best.items()]

        lp.pad = {}
        for o in all_body:
            for (k2, mode, w2) in o.lwaits:
                if mode == "prev":
                    C = lp.base[k2] + w2 - self.cnt[k2]
                    if C < 0:
                        lp.pad[k2] = max(lp.pad.get(k2, 0), -C)
        for k2, npad in lp.pad.items():
            assert k2.endswith(":c"), "cannot pad a DMA stream"
            assert lp.base[k2] > 0, f"prev-edge on {k2} with empty pre-stream"
            lp.base[k2] += npad

        for k in self.cnt:
            self.cnt[k] = lp.base[k] + lp.P.get(k, 0) * trip

        def conv(entry):
            if entry is not None and entry[0] == "loop" and entry[1] is lp:
                _, _, dep_key, w2 = entry
                return ("abs", dep_key,
                        lp.base[dep_key] + lp.P[dep_key] * (trip - 1) + w2)
            return entry
        for res in list(self.writers):
            self.writers[res] = conv(self.writers[res])
        for res in list(self.readers):
            for rk in list(self.readers[res]):
                self.readers[res][rk] = conv(self.readers[res][rk])
        for e in ENGINES:
            if lp.body[e]:
                self.items[e].append(("loop", lp))

    def finish(self):
        if self.defer:
            return
        assert not self.finished and self.cur_loop is None
        self.finished = True
        needed = {k: set() for k in self.sems}
        for e in ENGINES:
            for it in self.items[e]:
                if isinstance(it, _Op):
                    for k2, c2 in it.waits.items():
                        needed[k2].add(c2)
                else:
                    for k2, c2 in it[1].pre_waits[e].items():
                        needed[k2].add(c2)
        for e in ENGINES:
            last = {}
            for it in self.items[e]:
                if isinstance(it, _Op):
                    last[it.key] = it.cum
                else:
                    lp = it[1]
                    for o in lp.body[e]:
                        last[o.key] = lp.base[o.key] + lp.P[o.key] * lp.trip
            for k, c in last.items():
                needed[k].add(c)
        for e in ENGINES:
            prev = {f"{e}:c": 0, f"{e}:d": 0}
            targets = {k: sorted(needed[k]) for k in prev}
            tpos = {k: 0 for k in prev}
            last_op = {}
            for it in self.items[e]:
                if isinstance(it, _Op):
                    k = it.key
                    hit = False
                    while (tpos[k] < len(targets[k])
                           and targets[k][tpos[k]] <= it.cum):
                        hit = True
                        tpos[k] += 1
                    if it.inc == 16 or hit or getattr(it, "force", False):
                        it.carry = it.cum - prev[k]
                        prev[k] = it.cum
                    last_op[k] = it
                else:
                    lp = it[1]
                    for k in {o.key for o in lp.body[e]}:
                        pad = lp.pad.get(k, 0)
                        if prev[k] < lp.base[k]:
                            lo = last_op[k]
                            assert lo.cum + pad == lp.base[k], (
                                lo.cum, pad, lp.base[k])
                            lo.carry += lp.base[k] - prev[k]
                            prev[k] = lp.base[k]
                        end = lp.base[k] + lp.P[k] * lp.trip
                        while (tpos[k] < len(targets[k])
                               and targets[k][tpos[k]] <= end):
                            tpos[k] += 1
                        prev[k] = end
                    for o in lp.body[e]:
                        o.carry = o.inc

        for e in ENGINES:
            seen = {}
            for it in self.items[e]:
                if isinstance(it, _Op):
                    for k2 in list(it.waits):
                        if seen.get(k2, -1) >= it.waits[k2]:
                            del it.waits[k2]
                        else:
                            seen[k2] = it.waits[k2]
                else:
                    seen = {}
        blk = self.stack.enter_context(self.nc.Block())
        segs = self.segs

        def semof(k, c):
            lst = segs[k]
            for j in range(len(lst) - 1, -1, -1):
                if lst[j][0] < c:
                    return lst[j][1], c - lst[j][0]
            return lst[0][1], c
        for e in ENGINES:
            its = self.items[e]
            if not its:
                continue

            def body(eng, its=its, e=e):
                for it in its:
                    if isinstance(it, _Op):
                        for k2 in sorted(it.waits):
                            s2, v2 = semof(k2, it.waits[k2])
                            eng.wait_ge(s2, v2)
                        inst = it.fn(eng, None)
                        if it.carry:
                            s1, v1 = semof(it.key, it.cum)
                            assert v1 >= it.carry, (it.key, it.cum, it.carry)
                            inst.then_inc(s1, it.carry)
                    else:
                        lp = it[1]
                        for k2 in sorted(lp.pre_waits[e]):
                            s2, v2 = semof(k2, lp.pre_waits[e][k2])
                            eng.wait_ge(s2, v2)
                        with eng.Fori(0, lp.trip) as i:
                            for o in lp.body[e]:
                                for (k2, mode, w2) in o.lwaits:
                                    P = lp.P[k2]
                                    C = lp.base[k2] + (
                                        w2 if mode == "iter" else w2 - P)
                                    assert C >= 0, (k2, C)
                                    s2, C2 = semof(k2, C) if C > 0 else (
                                        segs[k2][-1][1], 0)
                                    eng.wait_ge(s2, i * P + C2)
                                inst = o.fn(eng, i)
                                if o.carry:
                                    s1, v1 = semof(o.key,
                                                   lp.base[o.key] + 1)
                                    inst.then_inc(s1, o.carry)
            getattr(blk, e)(body)



class Ctx:

    shared = None
    shared_prefix = ""
    shared_io = None

    def __init__(self, name):
        self.nc = bass.Bass("TRN2", target_bir_lowering=False, debug=False)
        self.stack = contextlib.ExitStack()
        self.eb = None
        self._pp = 0
        self.name = name
        self._stage_stacks = []
        self._inpc = {}
        self._intc = {}

    def __enter__(self):
        if Ctx.shared is not None:
            sh = Ctx.shared
            st = contextlib.ExitStack()
            sh._stage_stacks.append((sh.eb.stack, sh.eb.prefix, st))
            sh.eb.stack = st
            sh.eb.prefix = Ctx.shared_prefix
            sh.eb.defer = True
            return sh
        return self._enter_real()

    def _enter_real(self):
        self.stack.__enter__()
        self.eb = EB(self.nc, self.stack)
        eb = self.eb
        self.ps = [eb.psum(f"PS{i}", [128, TN], F32) for i in range(6)]
        self.pst = eb.psum("PST", [128, 128], BF16)
        self.psr = eb.psum("PSR", [128, TN], F32)
        self.pst2 = self.psr[:, 384:448].bitcast(BF16)
        self.ones1 = eb.sbuf("ones1", [1, 128], F32)
        eb.op("vector", lambda e, i: e.memset(self.ones1[:], 1.0),
              writes=[self.ones1])
        self.onesC = eb.sbuf("onesC", [128, 1], F32)
        eb.op("vector", lambda e, i: e.memset(self.onesC[:], 1.0),
              writes=[self.onesC])
        self.identb = eb.sbuf("identb", [128, 128], BF16)
        eb.op("gpsimd", lambda e, i: e.memset(self.identb[:], 1.0),
              writes=[self.identb])
        eb.op("gpsimd", lambda e, i: e.affine_select(
            out=self.identb[:], in_=self.identb[:], compare_op=ALU.is_equal,
            fill=0.0, base=0, pattern=[[-1, 128]], channel_multiplier=1),
            reads=[self.identb], writes=[self.identb])
        self._ccpool = [self.stack.enter_context(self.nc.sbuf_tensor(
            f"cc_{j}", [128, 1], F32)) for j in range(10)]
        return self

    def __exit__(self, *a):
        sh = Ctx.shared
        if sh is not None and sh._stage_stacks:
            old_stack, old_prefix, st = sh._stage_stacks.pop()
            st.close()
            sh.eb.stack = old_stack
            sh.eb.prefix = old_prefix
            for attr in ("maskd_", "maskU_", "ones2_", "sel2_"):
                if hasattr(sh, attr):
                    delattr(sh, attr)
            sh.eb.full_barrier(tag="stg")
            sh.eb.reset_streams(sh.stack)
            return False
        return self.stack.__exit__(*a)

    def constcol(self, value, p=128):
        key = float(value)
        if not hasattr(self, "_cc"):
            self._cc = {}
        if key not in self._cc:
            t = self._ccpool[len(self._cc)]
            self.eb.op("vector", lambda e, i, t=t, v=key: e.memset(t[:], v),
                       writes=[t])
            self._cc[key] = t
        return self._cc[key][0:p, 0:1]

    def pick_ps(self):
        p = self.ps[self._pp % len(self.ps)]
        self._pp += 1
        return p

    def inp(self, name, shape, dtype=F32):
        if Ctx.shared_io is not None and name in Ctx.shared_io:
            return Ctx.shared_io[name]
        if name not in self._inpc:
            self._inpc[name] = self.nc.dram_tensor(
                name, shape, dtype, kind="ExternalInput").ap()
        return self._inpc[name]

    def outp(self, name, shape, dtype=F32):
        if Ctx.shared_io is not None and name in Ctx.shared_io:
            return Ctx.shared_io[name]
        return self.nc.dram_tensor(name, shape, dtype, kind="ExternalOutput").ap()

    def internal(self, name, shape, dtype=F32):
        if name not in self._intc:
            self._intc[name] = self.nc.dram_tensor(
                name, shape, dtype, kind="Internal").ap()
        return self._intc[name]


def load_weight(ctx, name, hbm_ap, Din, Dout, dtype=BF16, queue="gpsimd"):
    kt = Din // 128
    eb = ctx.eb
    w = eb.sbuf(name, [128, kt * Dout], dtype)
    eb.dma(queue, w[:].rearrange("p (k m) -> p k m", k=kt),
           hbm_ap.rearrange("(k p) m -> p k m", p=128), writes=[w],
           tag=f"ld_{name}")
    return w


def wslice(w, Dout, kd, m0, mw=128):
    return w[:, kd * Dout + m0: kd * Dout + m0 + mw]


def load_sheet(ctx, name, hbm_ap, dtype=F32, queue="sync", kt=KT, t=T):
    eb = ctx.eb
    s = eb.sbuf(name, [128, kt * t], dtype)
    eb.dma(queue, s[:].rearrange("p (k t) -> p k t", k=kt),
           hbm_ap.rearrange("(k p) t -> p k t", p=128), writes=[s],
           tag=f"ld_{name}")
    return s


def sl(sheet, k, t0, tw=TN, t=T):
    return sheet[:, k * t + t0: k * t + t0 + tw]


def ln_alloc(ctx, tag="ln"):
    eb = ctx.eb
    return dict(
        sq=[eb.sbuf(f"{tag}_sq{j}", [128, TN], F32) for j in range(2)],
        mu=eb.sbuf(f"{tag}_mu", [1, TN], F32),
        ms=eb.sbuf(f"{tag}_ms", [1, TN], F32),
        tmp=eb.sbuf(f"{tag}_tmp", [128, TN], F32))


def ln_block(ctx, lns, xb, out, c, tag="ln", t=T, eps=1e-5, out_block=False):
    eb = ctx.eb
    t0 = 0 if out_block else c * TN
    to_ = TN if out_block else t
    sq, mu, ms, tmp = lns["sq"], lns["mu"], lns["ms"], lns["tmp"]
    p_s = ctx.psr
    for k in range(KT):
        eb.op("tensor", lambda e, i, k=k: e.matmul(
            p_s[0:1, 0:TN], ctx.onesC[:], xb[:, k * TN:(k + 1) * TN],
            start=(k == 0), stop=(k == KT - 1)),
            reads=[ctx.onesC, xb], writes=[p_s], tag=f"{tag}s")
    eb.op("vector", lambda e, i: e.tensor_scalar_mul(
        mu[:], p_s[0:1, 0:TN], 1.0 / D),
        reads=[p_s], writes=[mu])
    for k in range(KT):
        s = sq[k % 2]
        eb.op("scalar", lambda e, i, k=k, s=s: e.activation(
            s[:], xb[:, k * TN:(k + 1) * TN], AF.Square),
            reads=[xb], writes=[s], tag=f"{tag}sq")
        eb.op("tensor", lambda e, i, k=k, s=s: e.matmul(
            p_s[0:1, 0:TN], ctx.onesC[:], s[:],
            start=(k == 0), stop=(k == KT - 1)),
            reads=[s], writes=[p_s], tag=f"{tag}ss")
    eb.op("vector", lambda e, i: e.tensor_mul(ms[:], mu[:], mu[:]),
          reads=[mu], writes=[ms])
    eb.op("vector", lambda e, i: e.scalar_tensor_tensor(
        ms[:], p_s[0:1, 0:TN], 1.0 / D, ms[:], ALU.mult, ALU.subtract),
        reads=[p_s, ms], writes=[ms])
    epsc = ctx.constcol(eps, 1)
    eb.op("scalar", lambda e, i, b=epsc: e.activation(
        ms[:], ms[:], AF.Ln, bias=b), reads=[ms], writes=[ms])
    eb.op("scalar", lambda e, i: e.activation(
        ms[:], ms[:], AF.Exp, scale=-0.5), reads=[ms], writes=[ms])
    eb.op("vector", lambda e, i: e.tensor_mul(mu[:], mu[:], ms[:]),
          reads=[mu, ms], writes=[mu])
    eb.op("tensor", lambda e, i: e.matmul(
        ctx.psr[:, 0:TN], ctx.ones1[0:1, :], ms[:], start=True, stop=True),
        reads=[ctx.ones1, ms], writes=[ctx.psr], tag=f"{tag}b0")
    pb = ctx.pick_ps()
    eb.op("tensor", lambda e, i, pb=pb: e.matmul(
        pb[:, 0:TN], ctx.ones1[0:1, :], mu[:], start=True, stop=True),
        reads=[ctx.ones1, mu], writes=[pb], tag=f"{tag}b1")
    for k in range(KT):
        eb.op("vector", lambda e, i, k=k: e.tensor_mul(
            tmp[:], xb[:, k * TN:(k + 1) * TN], ctx.psr[:, 0:TN]),
            reads=[xb, ctx.psr], writes=[tmp], tag=f"{tag}m")
        eb.op("vector", lambda e, i, k=k, pb=pb, t0=t0, to_=to_: e.tensor_sub(
            sl(out, k, t0, t=to_), tmp[:], pb[:, 0:TN]),
            reads=[tmp, pb], writes=[out], tag=f"{tag}o")


def load_xblock(ctx, xb, x_hbm, c, tag="xb", res=None):
    ctx.eb.dma("sync", xb[:].rearrange("p (k t) -> p k t", k=KT),
               x_hbm.rearrange("(k p) t -> p k t", p=128)[:, :, c * TN:(c + 1) * TN],
               reads=[res] if res is not None else [],
               writes=[xb], tag=tag)


def store_xblock(ctx, xb, xo_hbm, c, tag="xo", res=None):
    ctx.eb.dma("sync", xo_hbm.rearrange("(k p) t -> p k t", p=128)[:, :, c * TN:(c + 1) * TN],
               xb[:].rearrange("p (k t) -> p k t", k=KT),
               reads=[xb], writes=[res] if res is not None else [], tag=tag)



def build_l0ab():
    with Ctx("l0ab") as ctx:
        eb = ctx.eb
        x = ctx.inp("x", [D, T])
        wqkv_h = ctx.inp("wqkv", [D, 3072])
        wo_h = ctx.inp("wo", [D, D])
        bo = ctx.inp("bo", [128, KT])
        ghi = ctx.inp("ghi", [128, KT])
        g2 = ctx.inp("g2", [2, KT])
        xo = ctx.outp("xo", [D, T])

        wq = load_weight(ctx, "wq_sb", wqkv_h, D, 3072)
        wo = load_weight(ctx, "wo_sb", wo_h, D, D)
        bo_t = eb.sbuf("bo_t", [128, KT], F32)
        ghi_t = eb.sbuf("ghi_t", [128, KT], F32)
        g2_t = eb.sbuf("g2_t", [2, KT], F32)
        eb.dma("sync", bo_t[:], bo, writes=[bo_t])
        eb.dma("sync", ghi_t[:], ghi, writes=[ghi_t])
        eb.dma("sync", g2_t[:], g2, writes=[g2_t])

        xn = eb.sbuf("xn", [128, KT * T], BF16)
        xb = eb.sbuf("xb", [128, KT * TN], F32)
        lns = ln_alloc(ctx)
        for c in range(NB):
            load_xblock(ctx, xb, x, c)
            ln_block(ctx, lns, xb, xn, c)

        ones2 = eb.sbuf("ones2", [128, 2], F32)
        eb.op("vector", lambda e, i: e.memset(ones2[:], 0.0), writes=[ones2])
        eb.op("vector", lambda e, i: e.memset(ones2[0:64, 0:1], 1.0),
              reads=[ones2], writes=[ones2])
        eb.op("vector", lambda e, i: e.memset(ones2[64:128, 1:2], 1.0),
              reads=[ones2], writes=[ones2])
        sel2 = eb.sbuf("sel2", [2, 128], F32)
        onesb2 = eb.sbuf("onesb2", [128, 2], BF16)
        eb.op("vector", lambda e, i: e.tensor_copy(onesb2[:], ones2[:]),
              reads=[ones2], writes=[onesb2])
        eb.op("tensor", lambda e, i: e.matmul(
            ctx.psr[0:2, 0:128], onesb2[:], ctx.identb[:],
            start=True, stop=True),
            reads=[onesb2, ctx.identb], writes=[ctx.psr])
        eb.op("vector", lambda e, i: e.tensor_copy(
            sel2[:], ctx.psr[0:2, 0:128]), reads=[ctx.psr], writes=[sel2])

        kvps = ctx.ps[5]
        kvb = [eb.sbuf(f"kvb{j}", [128, 2048], BF16) for j in range(2)]
        for nt in range(16):
            buf = kvb[nt % 2]
            for nch in range(4):
                ps = ctx.ps[nch % 4]
                for kd in range(KT):
                    eb.op("tensor", lambda e, i, nt=nt, nch=nch, kd=kd, ps=ps:
                          e.matmul(ps[:],
                                   xn[:, kd * T + nt * 128: kd * T + nt * 128 + 128],
                                   wq[:, kd * 3072 + 1024 + nch * TN:
                                      kd * 3072 + 1024 + nch * TN + TN],
                                   start=(kd == 0), stop=(kd == KT - 1)),
                          reads=[xn, wq], writes=[ps], tag="kvT")
                eb.op("scalar", lambda e, i, nch=nch, ps=ps, buf=buf: e.copy(
                    buf[:, nch * TN: nch * TN + TN], ps[:]),
                    reads=[ps], writes=[buf], tag="kvTe")
            for hp in range(8):
                for h2 in range(2):
                    h = hp * 2 + h2
                    eb.op("tensor", lambda e, i, h=h, nt=nt, h2=h2, hp=hp, buf=buf:
                          e.matmul(kvps[h2 * 64:h2 * 64 + 64, hp * 64:hp * 64 + 64],
                                   buf[:, h * 64: h * 64 + 64],
                                   buf[:, 1024 + h * 64: 1024 + h * 64 + 64],
                                   start=(nt == 0), stop=(nt == 15),
                                   tile_position=(0, h2 * 64)),
                          reads=[buf], writes=[kvps], tag="kv")

        kvn = eb.sbuf("kvn", [128, 8 * 64], BF16)
        sqs = eb.sbuf("sqs", [128, 64], F32)
        rs = eb.sbuf("rs", [128, 1], F32)
        kvf = eb.sbuf("kvf", [128, 64], F32)
        for hp in range(8):
            eb.op("vector", lambda e, i, hp=hp: e.tensor_copy(
                kvf[:], kvps[:, hp * 64:hp * 64 + 64]),
                reads=[kvps], writes=[kvf], tag="kve")
            eb.op("scalar", lambda e, i: e.activation(sqs[:], kvf[:], AF.Square),
                  reads=[kvf], writes=[sqs])
            eb.op("vector", lambda e, i: e.reduce_sum(rs[:], sqs[:], axis=AX.X),
                  reads=[sqs], writes=[rs])
            tiny = ctx.constcol(1e-30)
            eb.op("scalar", lambda e, i, b=tiny: e.activation(rs[:], rs[:], AF.Ln,
                                                              bias=b),
                  reads=[rs], writes=[rs])
            eb.op("scalar", lambda e, i: e.activation(rs[:], rs[:], AF.Exp,
                                                      scale=-0.5),
                  reads=[rs], writes=[rs])
            eb.op("vector", lambda e, i, hp=hp: e.tensor_scalar(
                kvn[:, hp * 64:hp * 64 + 64], kvf[:], rs[:, 0:1],
                ghi_t[:, hp:hp + 1], ALU.mult, ALU.mult),
                reads=[kvf, rs, ghi_t], writes=[kvn], tag="kvn")

        qs_ = eb.sbuf("qs_", [128, KT * TN], BF16)
        qn = eb.sbuf("qn", [128, KT * TN], BF16)
        sqb = eb.sbuf("sqb", [128, TN], F32)
        rrow = eb.sbuf("rrow", [2, TN], F32)
        ao = eb.sbuf("ao", [128, KT * TN], BF16)
        for c in range(NB):
            t0 = c * TN
            load_xblock(ctx, xb, x, c, tag="xb2")
            for m in range(KT):
                ps = ctx.ps[m % 4]
                for kd in range(KT):
                    eb.op("tensor", lambda e, i, m=m, kd=kd, ps=ps, t0=t0: e.matmul(
                        ps[:], wq[:, kd * 3072 + m * 128: kd * 3072 + m * 128 + 128],
                        sl(xn, kd, t0), start=(kd == 0), stop=(kd == KT - 1)),
                        reads=[xn, wq], writes=[ps], tag="q")
                eb.op("vector", lambda e, i, m=m, ps=ps: e.tensor_copy(
                    qs_[:, m * TN: m * TN + TN], ps[:]),
                    reads=[ps], writes=[qs_], tag="qe")
            for m in range(KT):
                p_s = ctx.psr
                eb.op("scalar", lambda e, i, m=m: e.activation(
                    sqb[:], qs_[:, m * TN:m * TN + TN], AF.Square),
                    reads=[qs_], writes=[sqb])
                eb.op("tensor", lambda e, i: e.matmul(
                    p_s[0:2, 0:TN], ones2[:], sqb[:], start=True, stop=True),
                    reads=[ones2, sqb], writes=[p_s], tag="qss")
                tiny = ctx.constcol(1e-30, 2)
                eb.op("scalar", lambda e, i, b=tiny: e.activation(
                    rrow[:], p_s[0:2, 0:TN], AF.Ln, bias=b),
                    reads=[p_s], writes=[rrow])
                eb.op("scalar", lambda e, i: e.activation(
                    rrow[:], rrow[:], AF.Exp, scale=-0.5),
                    reads=[rrow], writes=[rrow])
                eb.op("vector", lambda e, i, m=m: e.tensor_scalar_mul(
                    rrow[:], rrow[:], g2_t[:, m:m + 1]),
                    reads=[rrow, g2_t], writes=[rrow])
                pb = ctx.ps[m % 4]
                eb.op("tensor", lambda e, i, pb=pb: e.matmul(
                    pb[:, 0:TN], sel2[:], rrow[:], start=True, stop=True),
                    reads=[sel2, rrow], writes=[pb], tag="qb")
                eb.op("vector", lambda e, i, m=m, pb=pb: e.tensor_mul(
                    qn[:, m * TN:m * TN + TN], qs_[:, m * TN:m * TN + TN],
                    pb[:, 0:TN]),
                    reads=[qs_, pb], writes=[qn], tag="qn")
            for hp in range(8):
                ps = ctx.ps[hp % 4]
                for h2 in range(2):
                    eb.op("tensor", lambda e, i, h2=h2, hp=hp, ps=ps:
                          e.matmul(ps[h2 * 64:h2 * 64 + 64, 0:TN],
                                   kvn[h2 * 64:h2 * 64 + 64, hp * 64:hp * 64 + 64],
                                   qn[h2 * 64:h2 * 64 + 64, hp * TN:hp * TN + TN],
                                   start=True, stop=True,
                                   tile_position=(h2 * 64, h2 * 64)),
                          reads=[kvn, qn], writes=[ps], tag="at")
                eb.op("vector", lambda e, i, hp=hp, ps=ps: e.tensor_copy(
                    ao[:, hp * TN:hp * TN + TN], ps[:]),
                    reads=[ps], writes=[ao], tag="aoe")
            for m in range(KT):
                ps = ctx.ps[m % 4]
                for kd in range(KT):
                    eb.op("tensor", lambda e, i, m=m, kd=kd, ps=ps: e.matmul(
                        ps[:], wo[:, kd * D + m * 128: kd * D + m * 128 + 128],
                        ao[:, kd * TN: kd * TN + TN],
                        start=(kd == 0), stop=(kd == KT - 1)),
                        reads=[ao, wo], writes=[ps], tag="o")
                eb.op("vector", lambda e, i, m=m, ps=ps: e.scalar_tensor_tensor(
                    xb[:, m * TN:m * TN + TN], ps[:], bo_t[:, m:m + 1],
                    xb[:, m * TN:m * TN + TN], ALU.add, ALU.add),
                    reads=[ps, bo_t, xb], writes=[xb], tag="res")
            store_xblock(ctx, xb, xo, c)
        eb.barrier("sync")
        eb.finish()
    return ctx.nc



def build_moe(feat_io=False):
    E, I = 8, 4096
    IT = I // 128
    TT2 = 1024
    NB2 = TT2 // TN
    NCHB = TT2 // 256
    NTT = TT2 // 128
    with Ctx("moe") as ctx:
        eb = ctx.eb
        x = ctx.inp("x", [D, TT2])
        xTM = None if feat_io else ctx.inp("xTM", [TT2, D])
        gate_h = ctx.inp("gate_w", [D, E])
        gbrow_h = ctx.inp("gbrow", [1, E])
        upw_h = ctx.inp("up_w", [E, D, I])
        upb_h = ctx.inp("upb", [128, E * IT])
        dnw_h = ctx.inp("down_w", [E, I, D])
        dbal_h = ctx.inp("dbal", [2, 128, D])
        swu_h = ctx.inp("sw_up", [D, I])
        sbu_h = ctx.inp("sbu", [128, IT])
        swd_h = ctx.inp("sw_down", [I, D])
        sbdrow_h = ctx.inp("sbdrow", [1, D])
        xo = (ctx.outp("xo", [D, TT2]) if feat_io
              else ctx.outp("xoTM", [TT2, D]))

        gate = load_weight(ctx, "gate_sb", gate_h, D, E)
        gbrow = eb.sbuf("gbrow_s", [1, E], F32)
        upb = eb.sbuf("upb_s", [128, E * IT], F32)
        dbal = [eb.sbuf(f"dbal_s{j}", [128, D], BF16) for j in range(2)]
        sbu = eb.sbuf("sbu_s", [128, IT], F32)
        sbdrow = eb.sbuf("sbdrow_s", [1, D], BF16)
        eb.dma("sync", gbrow[:], gbrow_h, writes=[gbrow])
        eb.dma("sync", upb[:], upb_h, writes=[upb])
        eb.dma("gpsimd", dbal[0][:], dbal_h[0], writes=[dbal[0]])
        eb.dma("gpsimd", dbal[1][:], dbal_h[1], writes=[dbal[1]])
        eb.dma("sync", sbu[:], sbu_h, writes=[sbu])
        eb.dma("gpsimd", sbdrow[:], sbdrow_h, writes=[sbdrow])

        xn = eb.sbuf("xn", [128, KT * TT2], BF16)
        xb = eb.sbuf("xb", [128, KT * TN], F32)
        lns = ln_alloc(ctx)
        for c in range(NB2):
            load_xblock(ctx, xb, x, c)
            ln_block(ctx, lns, xb, xn, c, t=TT2)

        wt = eb.sbuf("wt", [128, NTT * E], F32)
        st8 = eb.sbuf("st8", [128, E], F32)
        m1 = eb.sbuf("m1", [128, 1], F32)
        m2 = eb.sbuf("m2", [128, 1], F32)
        msk = eb.sbuf("msk", [128, E], F32)
        stm = eb.sbuf("stm", [128, E], F32)
        neg = eb.sbuf("neg", [128, E], F32)
        eb.op("vector", lambda e, i: e.memset(neg[:], -1e30), writes=[neg])
        for tt in range(NTT):
            p_s = ctx.psr
            for kd in range(KT):
                eb.op("tensor", lambda e, i, kd=kd, tt=tt, E=E, TT2=TT2: e.matmul(
                    p_s[0:128, 0:E],
                    xn[:, kd * TT2 + tt * 128: kd * TT2 + tt * 128 + 128],
                    gate[:, kd * E:(kd + 1) * E],
                    start=(kd == 0), stop=False),
                    reads=[gate, xn], writes=[p_s], tag="rt")
            eb.op("tensor", lambda e, i, E=E: e.matmul(
                p_s[0:128, 0:E], ctx.ones1[0:1, :], gbrow[:],
                start=False, stop=True),
                reads=[ctx.ones1, gbrow], writes=[p_s], tag="rtb")
            eb.op("scalar", lambda e, i, E=E: e.activation(
                st8[:], p_s[0:128, 0:E], AF.Sigmoid),
                reads=[p_s], writes=[st8], tag="rtsig")
            eb.op("vector", lambda e, i: e.reduce_max(m1[:], st8[:], axis=AX.X),
                  reads=[st8], writes=[m1])
            eb.op("vector", lambda e, i: e.tensor_scalar(
                msk[:], st8[:], m1[:, 0:1], None, ALU.is_lt),
                reads=[st8, m1], writes=[msk])
            eb.op("vector", lambda e, i: e.tensor_scalar(
                stm[:], msk[:], 1.0, 1e30, ALU.subtract, ALU.mult),
                reads=[msk], writes=[stm])
            eb.op("vector", lambda e, i: e.tensor_add(stm[:], stm[:], st8[:]),
                  reads=[stm, st8], writes=[stm])
            eb.op("vector", lambda e, i, m2=m2: e.reduce_max(m2[:], stm[:], axis=AX.X),
                  reads=[stm], writes=[m2])
            eb.op("vector", lambda e, i, m2=m2: e.tensor_add(m1[:], m1[:], m2[:]),
                  reads=[m1, m2], writes=[m1])
            eb.op("vector", lambda e, i: e.tensor_scalar_add(m1[:], m1[:], 1e-6),
                  reads=[m1], writes=[m1])
            eb.op("vector", lambda e, i: e.reciprocal(m1[:], m1[:]),
                  reads=[m1], writes=[m1])
            eb.op("vector", lambda e, i: e.tensor_scalar_mul(m1[:], m1[:], 0.1),
                  reads=[m1], writes=[m1])
            eb.op("vector", lambda e, i, m2=m2: e.tensor_scalar(
                msk[:], st8[:], m2[:, 0:1], None, ALU.is_ge),
                reads=[st8, m2], writes=[msk])
            eb.op("vector", lambda e, i, tt=tt, E=E: e.tensor_mul(
                wt[:, tt * E:(tt + 1) * E], st8[:], msk[:]),
                reads=[st8, msk], writes=[wt])
            eb.op("vector", lambda e, i, tt=tt, E=E: e.tensor_scalar_mul(
                wt[:, tt * E:(tt + 1) * E], wt[:, tt * E:(tt + 1) * E],
                m1[:, 0:1]),
                reads=[wt, m1], writes=[wt])

        acc = eb.sbuf("acc", [128, NTT * D], BF16)
        for tt in range(NTT):
            eb.op("vector", lambda e, i, tt=tt: e.memset(
                acc[:, tt * D:(tt + 1) * D], 0.0), writes=[acc], tag="acc0")
        upw = eb.sbuf("upw", [128, KT * I], BF16)
        onesP = eb.sbuf("onesP", [128, 128], BF16)
        eb.op("vector", lambda e, i: e.memset(onesP[:], 1.0), writes=[onesP])
        dnw = eb.sbuf("dnw", [128, IT * D], BF16)
        gtile = eb.sbuf("gtile", [128, 256], BF16)
        tmpv = eb.sbuf("tmpv", [128, TN], F32)

        def expert_pass(ex):
            shared = ex < 0
            for chb in range(NCHB):
                t0 = chb * 256
                banks = [ctx.ps[2], ctx.ps[3], ctx.ps[4], ctx.ps[5]]
                for m in range(IT):
                    psu = ctx.ps[m % 2]
                    for kd in range(KT):
                        eb.op("tensor", lambda e, i, m=m, kd=kd, psu=psu, t0=t0, TT2=TT2, I=I:
                              e.matmul(psu[:, 0:256],
                                       upw[:, kd * I + m * 128: kd * I + m * 128 + 128],
                                       xn[:, kd * TT2 + t0: kd * TT2 + t0 + 256],
                                       start=(kd == 0), stop=(kd == KT - 1)),
                              reads=[upw, xn], writes=[psu], tag="up")
                    bcol = sbu[:, m:m + 1] if shared else                         upb[:, ex * IT + m: ex * IT + m + 1]
                    eb.op("scalar", lambda e, i, psu=psu, bcol=bcol: e.activation(
                        gtile[:], psu[:, 0:256], AF.Gelu, bias=bcol),
                        reads=[psu, sbu if shared else upb],
                        writes=[gtile], tag="gelu")
                    for tc in range(2):
                        for nch in range(2):
                            bk = banks[tc * 2 + nch]
                            eb.op("tensor", lambda e, i, m=m, tc=tc, nch=nch, bk=bk, IT=IT:
                                  e.matmul(bk[:, 0:TN],
                                           gtile[:, tc * 128:(tc + 1) * 128],
                                           dnw[:, m * D + nch * TN: m * D + nch * TN + TN],
                                           start=(m == 0), stop=(m == IT - 1)),
                                  reads=[gtile, dnw], writes=[bk], tag="dn")
                for tc in range(2):
                    tt = chb * 2 + tc
                    for nch in range(2):
                        bk = banks[tc * 2 + nch]
                        if shared:
                            dbr = sbdrow[:, nch * TN: nch * TN + TN]
                            dbread = sbdrow
                            ob = 0
                        else:
                            ob = 32 * (ex % 4)
                            dbr = dbal[ex // 4][ob:ob + 1,
                                                 nch * TN: nch * TN + TN]
                            dbread = dbal[ex // 4]
                        eb.op("tensor", lambda e, i, bk=bk, dbr=dbr, ob=ob:
                              e.matmul(bk[:, 0:TN], onesP[ob:ob + 1, :], dbr,
                                       start=False, stop=True,
                                       skip_group_check=True,
                                       tile_position=(ob, 0)),
                              reads=[onesP, dbread], writes=[bk], tag="db")
                        aslc = acc[:, tt * D + nch * TN: tt * D + nch * TN + TN]
                        if shared:
                            eb.op("vector", lambda e, i, bk=bk, aslc=aslc:
                                  e.tensor_scalar_mul(tmpv[:], bk[:, 0:TN], 0.1),
                                  reads=[bk], writes=[tmpv], tag="shs")
                            eb.op("vector", lambda e, i, aslc=aslc: e.tensor_add(
                                aslc, aslc, tmpv[:]),
                                reads=[acc, tmpv], writes=[acc], tag="sha")
                        else:
                            wcol = wt[:, tt * E + ex: tt * E + ex + 1]
                            eb.op("vector", lambda e, i, bk=bk, aslc=aslc, wcol=wcol:
                                  e.scalar_tensor_tensor(aslc, bk[:, 0:TN], wcol,
                                                         aslc, ALU.mult, ALU.add),
                                  reads=[bk, wt, acc], writes=[acc], tag="cmb")

        eb.dma("gpsimd", upw[:].rearrange("p (k m) -> p k m", k=KT),
               swu_h.rearrange("(k p) m -> p k m", p=128), writes=[upw],
               tag="ld_swu")
        eb.dma("gpsimd", dnw[:].rearrange("p (k m) -> p k m", k=IT),
               swd_h.rearrange("(k p) m -> p k m", p=128), writes=[dnw],
               tag="ld_swd")
        expert_pass(-1)
        for ex in range(E):
            eb.dma("gpsimd", upw[:].rearrange("p (k m) -> p k m", k=KT),
                   upw_h[ex].rearrange("(k p) m -> p k m", p=128),
                   writes=[upw], tag=f"ld_up{ex}")
            eb.dma("gpsimd", dnw[:].rearrange("p (k m) -> p k m", k=IT),
                   dnw_h[ex].rearrange("(k p) m -> p k m", p=128),
                   writes=[dnw], tag=f"ld_dn{ex}")
            expert_pass(ex)

        mx = eb.sbuf("mx", [128, 1], F32)
        xbTM = eb.sbuf("xbTM", [128, 128 if feat_io else TN], F32)
        otmb = eb.sbuf("otmb", [128, TN], BF16) if feat_io else None
        for tt in range(NTT):
            aslc = acc[:, tt * D:(tt + 1) * D]
            eb.op("vector", lambda e, i, aslc=aslc: e.reduce_max(
                mx[:], aslc, axis=AX.X, apply_absolute_value=True),
                reads=[acc], writes=[mx], tag="nmax")
            eb.op("vector", lambda e, i: e.tensor_scalar_add(mx[:], mx[:], 1e-6),
                  reads=[mx], writes=[mx])
            eb.op("vector", lambda e, i: e.reciprocal(mx[:], mx[:]),
                  reads=[mx], writes=[mx])
            for nh in range(2):
                eb.op("vector", lambda e, i, tt=tt, nh=nh: e.tensor_scalar_mul(
                    tmpv[:], acc[:, tt * D + nh * TN: tt * D + nh * TN + TN],
                    mx[:, 0:1]),
                    reads=[acc, mx], writes=[tmpv], tag="nsc")
                if not feat_io:
                    eb.dma("sync", xbTM[:],
                           xTM[tt * 128:(tt + 1) * 128, nh * TN:(nh + 1) * TN],
                           writes=[xbTM], tag="xtm")
                    eb.op("vector", lambda e, i: e.tensor_add(
                        xbTM[:], xbTM[:], tmpv[:]),
                        reads=[xbTM, tmpv], writes=[xbTM], tag="nres")
                    eb.dma("sync",
                           xo[tt * 128:(tt + 1) * 128, nh * TN:(nh + 1) * TN],
                           xbTM[:], reads=[xbTM], tag="xout")
                else:
                    eb.op("vector", lambda e, i: e.tensor_copy(
                        otmb[:], tmpv[:]),
                        reads=[tmpv], writes=[otmb], tag="ocp")
                    for cb in range(4):
                        kb = nh * 4 + cb
                        eb.op("tensor", lambda e, i, cb=cb: e.transpose(
                            ctx.pst[:], otmb[:, cb * 128:(cb + 1) * 128],
                            ctx.identb[:]),
                            reads=[otmb, ctx.identb], writes=[ctx.pst],
                            tag="otr")
                        eb.dma("sync", xbTM[:, 0:128],
                               x.rearrange("(k p) t -> p k t", p=128)
                               [:, kb, tt * 128:(tt + 1) * 128],
                               writes=[xbTM], tag="oxl")
                        eb.op("vector", lambda e, i: e.tensor_add(
                            xbTM[:, 0:128], xbTM[:, 0:128], ctx.pst[:]),
                            reads=[xbTM, ctx.pst], writes=[xbTM], tag="oad")
                        eb.dma("sync",
                               xo.rearrange("(k p) t -> p k t", p=128)
                               [:, kb, tt * 128:(tt + 1) * 128],
                               xbTM[:, 0:128], reads=[xbTM], tag="oso")
        eb.barrier("sync")
        eb.finish()
    return ctx.nc



def build_gru(dbg=False, scan_T=T):
    with Ctx("gru") as ctx:
        eb = ctx.eb
        nc = ctx.nc
        x = ctx.inp("x", [D, T])
        ins_h = {}
        for l in range(2):
            ins_h[f"lng{l}"] = ctx.inp(f"lng{l}", [128, KT])
            ins_h[f"lnb{l}"] = ctx.inp(f"lnb{l}", [128, KT])
            ins_h[f"winT{l}"] = ctx.inp(f"winT{l}", [D, 2 * D])
            ins_h[f"binu{l}"] = ctx.inp(f"binu{l}", [128, KT])
            ins_h[f"binv{l}"] = ctx.inp(f"binv{l}", [128, KT])
            ins_h[f"bh{l}"] = ctx.inp(f"bh{l}", [128, KT])
            ins_h[f"whT{l}"] = ctx.inp(f"whT{l}", [D, D])
            ins_h[f"woutT{l}"] = ctx.inp(f"woutT{l}", [D, D])
            ins_h[f"bout{l}"] = ctx.inp(f"bout{l}", [128, KT])
        xo = ctx.outp("xo", [D, T])
        vscr = ctx.internal("vscr", [D, T])
        if dbg:
            dxn = ctx.outp("dxn", [128, KT * TN])
            dv0 = ctx.outp("dv0", [D, T])
            dgate = ctx.outp("dgate", [128, KT * T])
            dvp = ctx.outp("dvp", [128, KT * T])
            dhs = ctx.outp("dhs", [128, KT * T])
            dv1 = ctx.outp("dv1", [D, T])

        win = eb.sbuf("win", [128, KT * 2 * D], BF16)
        wh = eb.sbuf("wh", [128, KT * D], BF16)
        wout = eb.sbuf("wout", [128, KT * D], BF16)
        sm = {}
        for nm in ["lng", "lnb", "binu", "binv", "bh", "bout"]:
            for l in range(2):
                sm[f"{nm}{l}"] = eb.sbuf(f"{nm}{l}_s", [128, KT], F32)
                eb.dma("sync", sm[f"{nm}{l}"][:], ins_h[f"{nm}{l}"],
                       writes=[sm[f"{nm}{l}"]])

        gate = eb.sbuf("gate", [128, KT * T], BF16)
        vp = eb.sbuf("vp", [128, KT * T], BF16)
        hs = eb.sbuf("hs", [128, KT * T], BF16)
        xb = eb.sbuf("xb", [128, KT * TN], F32)
        xnb = eb.sbuf("xnb", [128, KT * TN], BF16)
        h = eb.sbuf("h", [128, KT], F32)
        hb = eb.sbuf("hb", [128, KT], BF16)
        cpre = eb.sbuf("cpre", [128, KT], F32)
        lns = ln_alloc(ctx)
        psh = ctx.ps[5]

        if dbg:
            for kk in range(KT):
                eb.op("vector", lambda e, i, kk=kk: e.memset(
                    sl(hs, kk, 0, tw=T), 0.0), writes=[hs], tag="hs0")
                eb.op("vector", lambda e, i, kk=kk, g0=g0: e.memset(
                    sl(gate, kk, 0, tw=T), 0.0), writes=[gate], tag="g0")
                eb.op("vector", lambda e, i, kk=kk: e.memset(
                    sl(vp, kk, 0, tw=T), 0.0), writes=[vp], tag="vp0")
        gate3 = gate[:].rearrange("p (k t) -> p k t", k=KT)
        vp3 = vp[:].rearrange("p (k t) -> p k t", k=KT)
        hs3 = hs[:].rearrange("p (k t) -> p k t", k=KT)
        cpre3 = cpre[:].rearrange("p (k o) -> p k o", o=1)
        h3 = h[:].rearrange("p (k o) -> p k o", o=1)

        for l in range(2):
            eb.dma("gpsimd", win[:].rearrange("p (k m) -> p k m", k=KT),
                   ins_h[f"winT{l}"].rearrange("(k p) m -> p k m", p=128),
                   writes=[win], tag=f"ld_win{l}")
            eb.dma("gpsimd", wh[:].rearrange("p (k m) -> p k m", k=KT),
                   ins_h[f"whT{l}"].rearrange("(k p) m -> p k m", p=128),
                   writes=[wh], tag=f"ld_wh{l}")
            eb.dma("gpsimd", wout[:].rearrange("p (k m) -> p k m", k=KT),
                   ins_h[f"woutT{l}"].rearrange("(k p) m -> p k m", p=128),
                   writes=[wout], tag=f"ld_wout{l}")
            lng, lnb = sm[f"lng{l}"], sm[f"lnb{l}"]
            binu, binv, bh_, bout = (sm[f"binu{l}"], sm[f"binv{l}"],
                                     sm[f"bh{l}"], sm[f"bout{l}"])
            for c in range(NB):
                t0 = c * TN
                load_xblock(ctx, xb, x if l == 0 else vscr, c, tag=f"vin{l}", res=None if l == 0 else vscr)
                if l == 0:
                    ln_block(ctx, lns, xb, xb, c, out_block=True)
                    store_xblock(ctx, xb, vscr, c, tag="v0out", res=vscr)
                ln_block(ctx, lns, xb, xb, c, out_block=True)
                for k in range(KT):
                    eb.op("vector", lambda e, i, k=k, lng=lng, lnb=lnb:
                          e.tensor_scalar(
                              xnb[:, k * TN:(k + 1) * TN],
                              xb[:, k * TN:(k + 1) * TN],
                              lng[:, k:k + 1], lnb[:, k:k + 1],
                              ALU.mult, ALU.add),
                          reads=[xb, lng, lnb], writes=[xnb], tag="lngb")
                for m in range(16):
                    ps = ctx.ps[m % 4]
                    for kd in range(KT):
                        eb.op("tensor", lambda e, i, m=m, kd=kd, ps=ps: e.matmul(
                            ps[:], win[:, kd * 2 * D + m * 128: kd * 2 * D + m * 128 + 128],
                            xnb[:, kd * TN:(kd + 1) * TN],
                            start=(kd == 0), stop=(kd == KT - 1)),
                            reads=[win, xnb], writes=[ps], tag="uv")
                    if m < KT:
                        eb.op("scalar", lambda e, i, m=m, ps=ps, binu=binu, t0=t0:
                              e.activation(sl(gate, m, t0), ps[:], AF.Sigmoid,
                                           bias=binu[:, m:m + 1]),
                              reads=[ps, binu], writes=[gate], tag="uvg")
                    else:
                        eb.op("vector", lambda e, i, m=m, ps=ps, binv=binv, bh_=bh_, t0=t0:
                              e.tensor_scalar(sl(vp, m - KT, t0), ps[:],
                                              binv[:, m - KT:m - KT + 1],
                                              bh_[:, m - KT:m - KT + 1],
                                              ALU.add, ALU.add),
                              reads=[ps, binv, bh_], writes=[vp], tag="uvv")
            if dbg and l == 0:
                eb.dma("gpsimd", dv0, vscr, reads=[vscr], tag="dbgv0")
                eb.dma("gpsimd", dxn, xnb[:], reads=[xnb], tag="dbg0")
                eb.dma("gpsimd", dgate, gate[:], reads=[gate], tag="dbg1")
                eb.dma("gpsimd", dvp, vp[:], reads=[vp], tag="dbg2")
            eb.op("vector", lambda e, i, h=h: e.memset(h[:], 0.0), writes=[h])
            eb.op("vector", lambda e, i, h=h: e.tensor_copy(hb[:], h[:]),
                  reads=[h], writes=[hb])
            with eb.loop(scan_T):
                def mms(e, i):
                    for m in range(KT):
                        for kd in range(KT):
                            mm = e.matmul(psh[:, m:m + 1],
                                          wh[:, kd * D + m * 128: kd * D + m * 128 + 128],
                                          hb[:, kd:kd + 1],
                                          start=(kd == 0), stop=(kd == KT - 1))
                    return mm
                eb.op("tensor", mms, reads=[wh, hb], writes=[psh])
                eb.op("vector", lambda e, i, k=k: e.tensor_tensor(
                    cpre3, psh[:, 0:KT].rearrange("p (k o) -> p k o", o=1),
                    vp3[:, :, bass.ts(i, 1)], ALU.add),
                    reads=[psh, vp], writes=[cpre])
                eb.op("scalar", lambda e, i: e.activation(
                    cpre[:], cpre[:], AF.Sigmoid, scale=2.0),
                    reads=[cpre], writes=[cpre])
                eb.op("vector", lambda e, i: e.tensor_scalar(
                    cpre[:], cpre[:], 2.0, 1.0, ALU.mult, ALU.subtract),
                    reads=[cpre], writes=[cpre])
                eb.op("vector", lambda e, i, h=h: e.tensor_sub(cpre[:], cpre[:], h[:]),
                      reads=[cpre, h], writes=[cpre])
                eb.op("vector", lambda e, i: e.tensor_tensor(
                    cpre3, cpre3, gate3[:, :, bass.ts(i, 1)], ALU.mult),
                    reads=[cpre, gate], writes=[cpre])
                eb.op("vector", lambda e, i, h=h: e.tensor_add(h[:], h[:], cpre[:]),
                      reads=[h, cpre], writes=[h])
                eb.op("vector", lambda e, i, h=h: e.tensor_copy(
                    hs3[:, :, bass.ts(i, 1)], h3), reads=[h], writes=[hs])
                eb.op("vector", lambda e, i, h=h: e.tensor_copy(hb[:], h[:]),
                      reads=[h], writes=[hb])
            if dbg and l == 0:
                eb.dma("gpsimd", dhs, hs[:], reads=[hs], tag="dbg3")
            for c in range(NB):
                t0 = c * TN
                load_xblock(ctx, xb, vscr, c, tag=f"vre{l}", res=vscr)
                for m in range(KT):
                    ps = ctx.ps[m % 4]
                    for kd in range(KT):
                        eb.op("tensor", lambda e, i, m=m, kd=kd, ps=ps, t0=t0: e.matmul(
                            ps[:], wout[:, kd * D + m * 128: kd * D + m * 128 + 128],
                            sl(hs, kd, t0),
                            start=(kd == 0), stop=(kd == KT - 1)),
                            reads=[wout, hs], writes=[ps], tag="yp")
                    eb.op("vector", lambda e, i, m=m, ps=ps, bout=bout:
                          e.scalar_tensor_tensor(
                              xb[:, m * TN:(m + 1) * TN], ps[:],
                              bout[:, m:m + 1], xb[:, m * TN:(m + 1) * TN],
                              ALU.add, ALU.add),
                          reads=[ps, bout, xb], writes=[xb], tag="yres")
                store_xblock(ctx, xb, vscr, c, tag=f"vout{l}", res=vscr)
        if dbg:
            for c in range(NB):
                load_xblock(ctx, xb, vscr, c, tag="dbgv1", res=vscr)
                store_xblock(ctx, xb, dv1, c, tag="dbgv1o")
        tmp = lns["tmp"]
        xv = x.rearrange("(k p) t -> p k t", p=128)
        for c in range(NB):
            load_xblock(ctx, xb, vscr, c, tag="vfin", res=vscr)
            for k in range(KT):
                eb.dma("sync", tmp[:], xv[:, k, c * TN:(c + 1) * TN],
                       writes=[tmp], tag="xfin")
                eb.op("vector", lambda e, i, k=k: e.tensor_add(
                    xb[:, k * TN:(k + 1) * TN], xb[:, k * TN:(k + 1) * TN],
                    tmp[:]), reads=[xb, tmp], writes=[xb])
            store_xblock(ctx, xb, xo, c)
        eb.barrier("sync")
        eb.finish()
    return ctx.nc



def build_l1ab(dbg=False):
    H = 16
    with Ctx("l1ab") as ctx:
        eb = ctx.eb
        x = ctx.inp("x", [D, T])
        qa_h = ctx.inp("qa", [H, D, 64])
        ka_h = ctx.inp("ka", [H, D, 64])
        qbs_h = ctx.inp("qbs", [64, H * 64])
        kbs_h = ctx.inp("kbs", [64, H * 64])
        vwT_h = ctx.inp("vwT2", [D, D])
        vb_h = ctx.inp("vb", [128, KT])
        owT_h = ctx.inp("owT2", [D, D])
        ob_h = ctx.inp("ob", [128, KT])
        xo = ctx.outp("xo", [D, T])
        if dbg:
            dsim = ctx.outp("dsim", [128, 128])
            dp = ctx.outp("dp", [128, 128])
            dpT = ctx.outp("dpT", [128, 128])
            dob = ctx.outp("dob", [128, KT * 128])
            dq = ctx.outp("dq", [128, KT * T])
            dk = ctx.outp("dk", [128, KT * T])
            dvT = ctx.outp("dvT", [128, 16 * 16 * 65])

        vw = load_weight(ctx, "vw_sb", vwT_h, D, D)
        ow = load_weight(ctx, "ow_sb", owT_h, D, D)
        qbs = eb.sbuf("qbs_s", [64, H * 64], BF16)
        kbs = eb.sbuf("kbs_s", [64, H * 64], BF16)
        eb.dma("gpsimd", qbs[:], qbs_h, writes=[qbs])
        eb.dma("gpsimd", kbs[:], kbs_h, writes=[kbs])
        vb = eb.sbuf("vb_s", [128, KT], F32)
        ob = eb.sbuf("ob_s", [128, KT], F32)
        eb.dma("sync", vb[:], vb_h, writes=[vb])
        eb.dma("sync", ob[:], ob_h, writes=[ob])
        qab = [eb.sbuf(f"qab{j}", [128, KT * 64], BF16) for j in range(2)]
        kab = [eb.sbuf(f"kab{j}", [128, KT * 64], BF16) for j in range(2)]

        xn = eb.sbuf("xn", [128, KT * T], BF16)
        big = eb.sbuf("big", [128, KT * TN], F32)
        xb = big
        lns = ln_alloc(ctx)
        for c in range(NB):
            load_xblock(ctx, xb, x, c)
            ln_block(ctx, lns, xb, xn, c)

        qs = eb.sbuf("qs", [128, KT * T], BF16)
        ks = eb.sbuf("ks", [128, KT * T], BF16)
        midb = eb.sbuf("midb", [64, TN], BF16)
        for h in range(H):
            bq = qab[h % 2]
            bk = kab[h % 2]
            eb.dma("gpsimd", bq[:].rearrange("p (k m) -> p k m", k=KT),
                   qa_h[h].rearrange("(k p) m -> p k m", p=128),
                   writes=[bq], tag=f"qa{h}")
            eb.dma("gpsimd", bk[:].rearrange("p (k m) -> p k m", k=KT),
                   ka_h[h].rearrange("(k p) m -> p k m", p=128),
                   writes=[bk], tag=f"ka{h}")
            hb2 = (h % 2) * 64
            for qk, ab, bsheet, osheet in ((0, bq, qbs, qs), (1, bk, kbs, ks)):
                for c in range(NB):
                    t0 = c * TN
                    ps = ctx.ps[(2 * c + qk) % 4]
                    for kd in range(KT):
                        eb.op("tensor", lambda e, i, kd=kd, ab=ab, ps=ps, t0=t0:
                              e.matmul(ps[0:64, 0:TN],
                                       ab[:, kd * 64:(kd + 1) * 64],
                                       sl(xn, kd, t0),
                                       start=(kd == 0), stop=(kd == KT - 1)),
                              reads=[ab, xn], writes=[ps], tag="mid")
                    eb.op("scalar", lambda e, i, ps=ps: e.copy(
                        midb[:], ps[0:64, 0:TN]), reads=[ps], writes=[midb],
                        tag="mide")
                    ps2 = ctx.ps[4 + qk]
                    eb.op("tensor", lambda e, i, h=h, bsheet=bsheet, ps2=ps2:
                          e.matmul(ps2[0:64, 0:TN],
                                   bsheet[:, h * 64:(h + 1) * 64], midb[:],
                                   start=True, stop=True),
                          reads=[bsheet, midb], writes=[ps2], tag="qb")
                    qsc = 0.125 if qk == 0 else 1.0
                    eb.op("vector", lambda e, i, h=h, t0=t0, osheet=osheet,
                          ps2=ps2, hb2=hb2, qsc=qsc: e.tensor_scalar_mul(
                              osheet[:, :][hb2:hb2 + 64,
                                           (h // 2) * T + t0:(h // 2) * T + t0 + TN],
                              ps2[0:64, 0:TN], qsc),
                          reads=[ps2], writes=[osheet], tag="qbe")

        vT = eb.sbuf("vT", [128, 16 * H * 65], BF16)
        eb.op("vector", lambda e, i, H=H: e.memset(
            vT[:].rearrange("p (a h o) -> p a h o", a=16, h=H)[:, :, :, 64:65],
            1.0), writes=[vT], tag="vT1")
        vstg = eb.sbuf("vstg", [128, TN], BF16)
        for vt in range(KT):
            for c in range(NB):
                t0 = c * TN
                ps = ctx.ps[c % 4]
                for kd in range(KT):
                    eb.op("tensor", lambda e, i, vt=vt, kd=kd, ps=ps, t0=t0:
                          e.matmul(ps[:], vw[:, kd * D + vt * 128: kd * D + vt * 128 + 128],
                                   sl(xn, kd, t0),
                                   start=(kd == 0), stop=(kd == KT - 1)),
                          reads=[vw, xn], writes=[ps], tag="v")
                eb.op("vector", lambda e, i, vt=vt, ps=ps: e.tensor_scalar_add(
                    vstg[:], ps[:], vb[:, vt:vt + 1]),
                    reads=[ps, vb], writes=[vstg], tag="ve")
                for h2 in range(2):
                    h = vt * 2 + h2
                    for tc in range(4):
                        tt = c * 4 + tc
                        eb.op("tensor", lambda e, i, h2=h2, tc=tc: e.transpose(
                            ctx.pst[:, 0:64],
                            vstg[h2 * 64:h2 * 64 + 64, tc * 128:(tc + 1) * 128],
                            ctx.identb[h2 * 64:h2 * 64 + 64, h2 * 64:h2 * 64 + 64]),
                            reads=[vstg, ctx.identb], writes=[ctx.pst],
                            tag="vTt")
                        eb.op("vector", lambda e, i, tt=tt, h=h, H=H:
                              e.tensor_copy(
                                  vT[:, tt * H * 65 + h * 65: tt * H * 65 + h * 65 + 64],
                                  ctx.pst[:, 0:64]),
                              reads=[ctx.pst], writes=[vT], tag="vTe")

        simb = big[:, 0:T]
        pbuf = big[:, T:T + T // 2].bitcast(BF16)
        pT = big[:, T + T // 2: T + T].bitcast(BF16)
        rmax = eb.sbuf("rmax", [128, 1], F32)
        rrec = eb.sbuf("rrec", [1, 128], F32)
        ostg = eb.sbuf("ostg", [64, 128], F32)
        oblk = eb.sbuf("oblk", [128, KT * 128], BF16)
        xc = eb.sbuf("xc", [128, KT * 128], F32)
        maskd = eb.sbuf("maskd", [128, 128], BF16)
        eb.op("gpsimd", lambda e, i: e.memset(maskd[:], 0.0), writes=[maskd])
        eb.op("gpsimd", lambda e, i: e.affine_select(
            out=maskd[:], in_=maskd[:], compare_op=ALU.is_ge, fill=-1e30,
            base=0, pattern=[[-1, 128]], channel_multiplier=1),
            reads=[maskd], writes=[maskd])

        xv = x.rearrange("(k p) t -> p k t", p=128)
        xov = xo.rearrange("(k p) t -> p k t", p=128)
        for ib in range(16):
            jw = (ib + 1) * 128
            nch = (jw + TN - 1) // TN
            for h in range(H):
                hb2 = (h % 2) * 64
                hrow = (h // 2)
                for ch in range(nch):
                    cw = min(TN, jw - ch * TN)
                    ps = ctx.ps[ch % 4]
                    diag = (ch == nch - 1)
                    eb.op("tensor", lambda e, i, ib=ib, h=h, ch=ch, cw=cw,
                          ps=ps, hb2=hb2, hrow=hrow: e.matmul(
                              ps[0:128, 0:cw],
                              qs[hb2:hb2 + 64,
                                 hrow * T + ib * 128: hrow * T + ib * 128 + 128],
                              ks[hb2:hb2 + 64,
                                 hrow * T + ch * TN: hrow * T + ch * TN + cw],
                              start=True, stop=True),
                          reads=[qs, ks], writes=[ps], tag="sim")
                    if diag:
                        eb.op("tensor", lambda e, i, cw=cw, ps=ps: e.matmul(
                            ps[0:128, cw - 128:cw], ctx.identb[:], maskd[:],
                            start=False, stop=True, skip_group_check=True),
                            reads=[ctx.identb, maskd], writes=[ps], tag="simm")
                    eb.op("scalar", lambda e, i, ch=ch, cw=cw, ps=ps: e.copy(
                        simb[:, ch * TN: ch * TN + cw], ps[0:128, 0:cw]),
                        reads=[ps], writes=[big], tag="sime")
                eb.op("vector", lambda e, i, jw=jw: e.reduce_max(
                    rmax[:], simb[:, 0:jw], axis=AX.X, negate=True),
                    reads=[big], writes=[rmax], tag="rmax")
                eb.op("scalar", lambda e, i, jw=jw: e.activation(
                    pbuf[:, 0:jw], simb[:, 0:jw], AF.Exp, bias=rmax[:, 0:1]),
                    reads=[big, rmax], writes=[big], tag="pexp")
                for jt in range(ib + 1):
                    pstP = ctx.pst if jt % 2 == 0 else ctx.pst2
                    kPS = ctx.pst if jt % 2 == 0 else ctx.psr
                    eb.op("tensor", lambda e, i, jt=jt, pstP=pstP: e.transpose(
                        pstP[:], pbuf[:, jt * 128:(jt + 1) * 128],
                        ctx.identb[:]),
                        reads=[big, ctx.identb], writes=[kPS], tag="pTt")
                    eb.op("vector", lambda e, i, jt=jt, pstP=pstP: e.tensor_copy(
                        pT[:, jt * 128:(jt + 1) * 128], pstP[:]),
                        reads=[kPS], writes=[big], tag="pTe")
                psA = ctx.ps[4]
                for jt in range(ib + 1):
                    eb.op("tensor", lambda e, i, jt=jt, h=h, ib=ib, psA=psA,
                          H=H: e.matmul(
                              psA[0:65, 0:128],
                              vT[:, jt * H * 65 + h * 65: jt * H * 65 + h * 65 + 65],
                              pT[:, jt * 128:(jt + 1) * 128],
                              start=(jt == 0), stop=(jt == ib)),
                          reads=[vT, big], writes=[psA], tag="av")
                eb.op("vector", lambda e, i, psA=psA: e.reciprocal(
                    rrec[:], psA[64:65, 0:128]), reads=[psA], writes=[rrec],
                    tag="orc")
                psB = ctx.ps[5]
                eb.op("tensor", lambda e, i, psB=psB: e.matmul(
                    psB[0:64, 0:128], ctx.ones1[0:1, 0:64], rrec[:],
                    start=True, stop=True),
                    reads=[ctx.ones1, rrec], writes=[psB], tag="obc")
                eb.op("scalar", lambda e, i, psA=psA: e.copy(
                    ostg[:], psA[0:64, 0:128]), reads=[psA], writes=[ostg],
                    tag="oste")
                eb.op("vector", lambda e, i, h=h, psB=psB, hb2=hb2, hrow=hrow:
                      e.tensor_mul(
                          oblk[:, :][hb2:hb2 + 64, hrow * 128:(hrow + 1) * 128],
                          ostg[:], psB[0:64, 0:128]),
                      reads=[ostg, psB], writes=[oblk], tag="osc")
            if dbg and ib == 0:
                eb.dma("gpsimd", dsim, simb[:, 0:128], reads=[big], tag="dd1")
                eb.dma("gpsimd", dp, pbuf[:, 0:128], reads=[big], tag="dd2")
                eb.dma("gpsimd", dpT, pT[:, 0:128], reads=[big], tag="dd3")
                eb.dma("gpsimd", dob, oblk[:], reads=[oblk], tag="dd4")
                eb.dma("gpsimd", dq, qs[:], reads=[qs], tag="dd5")
                eb.dma("gpsimd", dk, ks[:], reads=[ks], tag="dd6")
                eb.dma("gpsimd", dvT, vT[:], reads=[vT], tag="dd7")
            eb.dma("sync", xc[:].rearrange("p (k t) -> p k t", k=KT),
                   xv[:, :, ib * 128:(ib + 1) * 128], writes=[xc], tag="xc")
            for m in range(KT):
                ps = ctx.ps[m % 4]
                for kd in range(KT):
                    eb.op("tensor", lambda e, i, m=m, kd=kd, ps=ps: e.matmul(
                        ps[0:128, 0:128],
                        ow[:, kd * D + m * 128: kd * D + m * 128 + 128],
                        oblk[:, kd * 128:(kd + 1) * 128],
                        start=(kd == 0), stop=(kd == KT - 1)),
                        reads=[ow, oblk], writes=[ps], tag="op")
                eb.op("vector", lambda e, i, m=m, ps=ps: e.scalar_tensor_tensor(
                    xc[:, m * 128:(m + 1) * 128], ps[0:128, 0:128],
                    ob[:, m:m + 1], xc[:, m * 128:(m + 1) * 128],
                    ALU.add, ALU.add),
                    reads=[ps, ob, xc], writes=[xc], tag="ores")
            eb.dma("sync", xov[:, :, ib * 128:(ib + 1) * 128],
                   xc[:].rearrange("p (k t) -> p k t", k=KT),
                   reads=[xc], tag="oxo")
        eb.barrier("sync")
        eb.finish()
    return ctx.nc



def build_l1ffn(dbg=False):
    H8 = 8
    GT = 8
    TP = T + GT
    W1M = 22
    with Ctx("l1ffn") as ctx:
        eb = ctx.eb
        nc = ctx.nc
        x = ctx.inp("x", [D, T])
        ih = {}
        for l in range(2):
            ih[f"wqkv{l}"] = ctx.inp(f"wqkv{l}", [D, 1536])
            ih[f"qsc{l}"] = ctx.inp(f"qsc{l}", [128, 1])
            ih[f"ksc{l}"] = ctx.inp(f"ksc{l}", [128, 1])
            ih[f"qss{l}"] = ctx.inp(f"qss{l}", [128, 1])
            ih[f"kss{l}"] = ctx.inp(f"kss{l}", [128, 1])
            ih[f"lg{l}"] = ctx.inp(f"lg{l}", [128, KT])
            ih[f"lb{l}"] = ctx.inp(f"lb{l}", [128, KT])
            ih[f"w1p{l}"] = ctx.inp(f"w1p{l}", [D, W1M * 2 * 128])
            ih[f"b1a{l}"] = ctx.inp(f"b1a{l}", [128, W1M])
            ih[f"b1g{l}"] = ctx.inp(f"b1g{l}", [128, W1M])
            ih[f"w2p{l}"] = ctx.inp(f"w2p{l}", [W1M * 128, D])
            ih[f"b2{l}"] = ctx.inp(f"b2{l}", [128, KT])
            ih[f"wo{l}"] = ctx.inp(f"wo{l}", [512, D])
        xo = ctx.outp("xo", [D, T])
        vscr = ctx.internal("vscr", [D, T])

        xsheet = eb.sbuf("xsheet", [128, KT * TP], BF16)
        SCR = eb.sbuf("SCR", [128, 22528], F32)
        wqkv = eb.sbuf("wqkv_s", [128, KT * 1536], BF16)
        wo = eb.sbuf("wo_s", [128, 4 * D], BF16)
        w1a = [eb.sbuf(f"w1a{j}", [128, KT * 128], BF16) for j in range(2)]
        w1g = [eb.sbuf(f"w1g{j}", [128, KT * 128], BF16) for j in range(2)]
        cs_q = eb.sbuf("cs_q", [128, T], BF16)
        ss_q = eb.sbuf("ss_q", [128, T], BF16)
        cs_k = eb.sbuf("cs_k", [128, T], BF16)
        ss_k = eb.sbuf("ss_k", [128, T], BF16)
        sm = {}
        for nm in ["qsc", "ksc", "qss", "kss", "lg", "lb", "b1a", "b1g", "b2"]:
            for l in range(2):
                key = f"{nm}{l}"
                shp = [128, 1] if nm in ("qsc", "ksc", "qss", "kss") else (
                    [128, W1M] if nm in ("b1a", "b1g") else [128, KT])
                sm[key] = eb.sbuf(key + "_s", shp, F32)
                eb.dma("sync", sm[key][:], ih[key], writes=[sm[key]])
        xcnk = eb.sbuf("xcnk", [128, KT * 128], F32)
        _sq0 = eb.sbuf("l_sq0", [128, TN], F32)
        lns = dict(sq=[_sq0, _sq0],
                   mu=eb.sbuf("l_mu", [1, TN], F32),
                   ms=eb.sbuf("l_ms", [1, TN], F32),
                   tmp=eb.sbuf("l_tmp", [128, TN], F32))
        xb = eb.sbuf("xb", [128, KT * TN], F32)

        def scr_bf16(off_f32, n_f32):
            return SCR[:, off_f32:off_f32 + n_f32].bitcast(BF16)
        qrot = scr_bf16(0, 4096)
        krot = scr_bf16(4096, 4096)
        vsh = scr_bf16(8192, 4096)
        vT = scr_bf16(12288, 4160)
        simb = SCR[:, 16448:17088]
        pbufA = scr_bf16(17088, 320)
        pTA = scr_bf16(17408, 320)
        hbuf = scr_bf16(0, 11264)
        w2p = scr_bf16(11264, 11264)

        hidx = eb.sbuf("hidx", [128, 1], F32)
        invf = eb.sbuf("invf", [128, 1], F32)
        lnb_ = eb.sbuf("lnb_", [128, 1], F32)
        trow = lns["ms"]
        fch = SCR[:, 0:TN]
        sch = SCR[:, TN:2 * TN]
        mch = SCR[:, 2 * TN:3 * TN]
        for g0 in (0, 32, 64, 96):
            eb.op("gpsimd", lambda e, i, g0=g0: e.iota(
                hidx[g0:g0 + 32, 0:1], pattern=[[0, 1]],
                base=0, channel_multiplier=2,
                allow_small_or_imprecise_dtypes=True),
                reads=[hidx], writes=[hidx], tag="hid")
        import math
        eb.op("scalar", lambda e, i: e.activation(
            invf[:], hidx[:], AF.Exp, scale=-math.log(10000.0) / 64.0),
            reads=[hidx], writes=[invf])
        b256 = ctx.constcol(25.6)
        eb.op("scalar", lambda e, i, b=b256: e.activation(
            lnb_[:], hidx[:], AF.Ln, bias=b, scale=1.0),
            reads=[hidx], writes=[lnb_], tag="lnb1")
        eb.op("vector", lambda e, i: e.tensor_scalar_add(
            lnb_[:], lnb_[:], -math.log(89.6)), reads=[lnb_], writes=[lnb_])

        TWO_PI = 2 * math.pi
        for tc in range(NB):
            t0 = tc * TN
            eb.op("gpsimd", lambda e, i, t0=t0: e.iota(
                trow[:], pattern=[[1, TN]], base=t0, channel_multiplier=0,
                allow_small_or_imprecise_dtypes=True),
                reads=[trow], writes=[trow], tag="tiota")
            pb = ctx.pick_ps()
            eb.op("tensor", lambda e, i, pb=pb: e.matmul(
                pb[:, 0:TN], ctx.ones1[0:1, :], trow[:],
                start=True, stop=True),
                reads=[ctx.ones1, trow], writes=[pb], tag="tb")
            eb.op("vector", lambda e, i, pb=pb: e.tensor_scalar_mul(
                fch, pb[:, 0:TN], invf[:, 0:1]),
                reads=[pb, invf], writes=["fch"], tag="freq")
            for trig, outsheet, shift in ((0, None, 0.0),):
                pass
            kint = lns["sq"][0]
            ich = kint[:].bitcast(mybir.dt.int32)

            def range_reduce_sin(dst, src, phase, reads, tag):
                eb.op("vector", lambda e, i, phase=phase, src=src:
                      e.tensor_scalar_add(mch, src, phase),
                      reads=reads, writes=["mch"], tag=tag + "m")
                eb.op("vector", lambda e, i, tp=TWO_PI: e.tensor_scalar_mul(
                    sch, mch, 1.0 / tp),
                    reads=["mch"], writes=["sch"], tag=tag + "c")
                eb.op("vector", lambda e, i: e.tensor_copy(ich, sch),
                      reads=["sch"], writes=[kint], tag=tag + "k")
                eb.op("vector", lambda e, i: e.tensor_copy(sch, ich),
                      reads=[kint], writes=["sch"], tag=tag + "f")
                eb.op("vector", lambda e, i, tp=TWO_PI: e.scalar_tensor_tensor(
                    mch, sch, -tp, mch, ALU.mult, ALU.add),
                    reads=["sch", "mch"], writes=["mch"], tag=tag + "s")
                eb.op("scalar", lambda e, i, dst=dst: e.activation(
                    dst, mch, AF.Sin), reads=["mch"], writes=["rotsheets"],
                    tag=tag + "sin")
            pwr = fch
            pw2 = eb.sbuf(f"pw2_{tc}", [128, TN], F32) if False else None
            sclq = lns["tmp"]
            eb.op("vector", lambda e, i, pb=pb: e.tensor_scalar(
                sclq[:], pb[:, 0:TN], 1024.0, 1.0 / 256.0,
                ALU.subtract, ALU.mult),
                reads=[pb], writes=[sclq], tag="pwr")
            eb.op("vector", lambda e, i: e.tensor_scalar_mul(
                sclq[:], sclq[:], lnb_[:, 0:1]),
                reads=[sclq, lnb_], writes=[sclq], tag="pwl")
            eb.op("scalar", lambda e, i: e.activation(
                sclq[:], sclq[:], AF.Exp), reads=[sclq], writes=[sclq],
                tag="pse")
            range_reduce_sin(cs_q[:, t0:t0 + TN], fch, math.pi / 2,
                             ["fch"], "cos")
            range_reduce_sin(ss_q[:, t0:t0 + TN], fch, 0.0, ["fch"], "sin")
            eb.op("vector", lambda e, i, t0=t0: e.tensor_mul(
                cs_k[:, t0:t0 + TN], cs_q[:, t0:t0 + TN], sclq[:]),
                reads=["rotsheets", sclq], writes=["rotsheets"], tag="csk0")
            eb.op("vector", lambda e, i, t0=t0: e.tensor_mul(
                ss_k[:, t0:t0 + TN], ss_q[:, t0:t0 + TN], sclq[:]),
                reads=["rotsheets", sclq], writes=["rotsheets"], tag="ssk0")
            eb.op("vector", lambda e, i: e.reciprocal(sclq[:], sclq[:]),
                  reads=[sclq], writes=[sclq], tag="pinv")
            eb.op("vector", lambda e, i, t0=t0: e.tensor_mul(
                cs_q[:, t0:t0 + TN], cs_q[:, t0:t0 + TN], sclq[:]),
                reads=["rotsheets", sclq], writes=["rotsheets"], tag="csq??")
            eb.op("vector", lambda e, i, t0=t0: e.tensor_mul(
                ss_q[:, t0:t0 + TN], ss_q[:, t0:t0 + TN], sclq[:]),
                reads=["rotsheets", sclq], writes=["rotsheets"], tag="ssq??")
        eb.full_barrier(tag="fence_tables")
        csQ, ssQ, csK, ssK = cs_k, ss_k, cs_q, ss_q
        pswap = scr_bf16(22464, 64)

        def build_pswap():
            eb.op("gpsimd", lambda e, i: e.memset(pswap[:], 0.0),
                  reads=["w2p"], writes=["w2p"], tag="pm0")
            for g0, c0, val in ((0, 32, 1.0), (32, 0, -1.0),
                                (64, 96, 1.0), (96, 64, -1.0)):
                eb.op("gpsimd", lambda e, i, g0=g0, c0=c0, val=val: e.memset(
                    pswap[g0:g0 + 32, c0:c0 + 32], val),
                    reads=["w2p"], writes=["w2p"], tag="pmm")
                eb.op("gpsimd", lambda e, i, g0=g0, c0=c0: e.affine_select(
                    out=pswap[g0:g0 + 32, c0:c0 + 32],
                    in_=pswap[g0:g0 + 32, c0:c0 + 32],
                    compare_op=ALU.is_equal, fill=0.0,
                    base=0, pattern=[[-1, 32]], channel_multiplier=1),
                    reads=["w2p"], writes=["w2p"], tag="pma")

        def upd_sheet(xfrom, cols0, width, tagu, rk=None):
            for k in range(KT):
                eb.op("vector", lambda e, i, k=k, cols0=cols0, width=width,
                      GT=GT, TP=TP, xfrom=xfrom:
                      e.tensor_copy(
                          xsheet[:, k * TP + GT + cols0:
                                 k * TP + GT + cols0 + width],
                          xfrom[:, k * width:(k + 1) * width]),
                      reads=[rk if rk is not None else xfrom],
                      writes=[xsheet], tag=tagu)

        for k in range(KT):
            eb.op("vector", lambda e, i, k=k, GT=GT, TP=TP: e.memset(
                xsheet[:, k * TP: k * TP + GT], 0.0), writes=[xsheet],
                tag="xg0")
        for c in range(NB):
            load_xblock(ctx, xb, x, c)
            ln_block(ctx, lns, xb, xb, c, out_block=True)
            store_xblock(ctx, xb, vscr, c, tag="v0o", res=vscr)
            upd_sheet(xb, c * TN, TN, "xs0")

        qsheet = qrot
        for l in range(2):
            build_pswap()
            eb.dma("gpsimd", wqkv[:].rearrange("p (k m) -> p k m", k=KT),
                   ih[f"wqkv{l}"].rearrange("(k p) m -> p k m", p=128),
                   writes=[wqkv], tag=f"wqkv{l}")
            eb.dma("gpsimd", wo[:].rearrange("p (k m) -> p k m", k=4),
                   ih[f"wo{l}"].rearrange("(k p) m -> p k m", p=128),
                   writes=[wo], tag=f"wo{l}")
            qsc, ksc = sm[f"qsc{l}"], sm[f"ksc{l}"]
            for m in range(12):
                for c in range(NB):
                    t0 = c * TN
                    ps = ctx.ps[(m * NB + c) % 4]
                    for kd in range(KT):
                        off = GT + t0 - (1 if kd >= 4 else 0)
                        eb.op("tensor", lambda e, i, m=m, kd=kd, ps=ps, off=off, TP=TP:
                              e.matmul(ps[:],
                                       wqkv[:, kd * 1536 + m * 128:
                                            kd * 1536 + m * 128 + 128],
                                       xsheet[:, kd * TP + off:
                                              kd * TP + off + TN],
                                       start=(kd == 0), stop=(kd == KT - 1)),
                              reads=[wqkv, xsheet], writes=[ps], tag="qkv")
                    if m < 8:
                        dst, raw = (qrot, 0) if m < 4 else (krot, 0)
                        mm = m % 4
                        eb.op("vector", lambda e, i, m=m, mm=mm, t0=t0, ps=ps,
                              dst=dst: e.tensor_copy(
                                  dst[:, mm * T + t0: mm * T + t0 + TN], ps[:]),
                              reads=[ps], writes=["rawqk"], tag="qkve")
                    else:
                        mm = m - 8
                        eb.op("vector", lambda e, i, mm=mm, t0=t0, ps=ps:
                              e.tensor_copy(
                                  vsh[:, mm * T + t0: mm * T + t0 + TN], ps[:]),
                              reads=[ps], writes=["vsh"], tag="ve")
            sq = lns["sq"][0]
            rrow2 = lns["tmp"][0:2, 0:TN]
            rt1 = xb[:, 0:TN]
            rt2 = xb[:, TN:2 * TN]
            rt3 = xb[:, 2 * TN:3 * TN]
            ones2 = getattr(ctx, "ones2_", None)
            if ones2 is None:
                ones2 = eb.sbuf("ones2", [128, 2], F32)
                eb.op("vector", lambda e, i, ones2=ones2: e.memset(ones2[:], 0.0),
                      writes=[ones2])
                eb.op("vector", lambda e, i, ones2=ones2: e.memset(
                    ones2[0:64, 0:1], 1.0), reads=[ones2], writes=[ones2])
                eb.op("vector", lambda e, i, ones2=ones2: e.memset(
                    ones2[64:128, 1:2], 1.0), reads=[ones2], writes=[ones2])
                sel2 = eb.sbuf("sel2", [2, 128], F32)
                onesb2 = eb.sbuf("onesb2", [128, 2], BF16)
                eb.op("vector", lambda e, i, onesb2=onesb2, ones2=ones2:
                      e.tensor_copy(onesb2[:], ones2[:]),
                      reads=[ones2], writes=[onesb2])
                eb.op("tensor", lambda e, i, onesb2=onesb2: e.matmul(
                    ctx.psr[0:2, 0:128], onesb2[:], ctx.identb[:],
                    start=True, stop=True),
                    reads=[onesb2, ctx.identb], writes=[ctx.psr])
                eb.op("vector", lambda e, i, sel2=sel2: e.tensor_copy(
                    sel2[:], ctx.psr[0:2, 0:128]),
                    reads=[ctx.psr], writes=[sel2])
                ctx.ones2_ = ones2
                ctx.sel2_ = sel2
            sel2 = ctx.sel2_
            qss, kss = sm[f"qss{l}"], sm[f"kss{l}"]
            for qk, sheet, scol, sscol, cst, sst in (
                    (0, qrot, qsc, qss, csQ, ssQ),
                    (1, krot, ksc, kss, csK, ssK)):
                for mt in range(4):
                    for c in range(NB):
                        t0 = c * TN
                        dbg_tile = dbg and l == 0 and qk == 0 and mt == 0 \
                            and c == 0

                        def ddump(nm, src, rkeys):
                            dout = ctx.outp(nm, [128, TN], F32)
                            eb.op("vector", lambda e, i, src=src:
                                  e.tensor_copy(lns["tmp"][:], src),
                                  reads=rkeys, writes=[lns["tmp"]],
                                  tag="dcp" + nm)
                            eb.dma("sync", dout, lns["tmp"][:],
                                   reads=[lns["tmp"]], tag="dd" + nm)
                        eb.op("scalar", lambda e, i, mt=mt, t0=t0, sheet=sheet:
                              e.activation(sq[:],
                                           sheet[:, mt * T + t0: mt * T + t0 + TN],
                                           AF.Square),
                              reads=["rawqk"], writes=[sq], tag="l2sq")
                        eb.op("tensor", lambda e, i, ones2=ones2: e.matmul(
                            ctx.psr[0:2, 0:TN], ones2[:], sq[:],
                            start=True, stop=True),
                            reads=[ones2, sq], writes=[ctx.psr], tag="l2ss")
                        tiny2 = ctx.constcol(1e-38, 2)
                        eb.op("scalar", lambda e, i, b=tiny2: e.activation(
                            rrow2, ctx.psr[0:2, 0:TN], AF.Ln, bias=b),
                            reads=[ctx.psr], writes=[lns["tmp"]], tag="l2ln")
                        eb.op("scalar", lambda e, i: e.activation(
                            rrow2, rrow2, AF.Exp, scale=-0.5),
                            reads=[lns["tmp"]], writes=[lns["tmp"]], tag="l2ex")
                        eb.op("vector", lambda e, i: e.tensor_scalar_min(
                            rrow2, rrow2, 1e12),
                            reads=[lns["tmp"]], writes=[lns["tmp"]], tag="l2cl")
                        pb = ctx.ps[4]
                        eb.op("tensor", lambda e, i, pb=pb, sel2=sel2: e.matmul(
                            pb[:, 0:TN], sel2[:], rrow2,
                            start=True, stop=True),
                            reads=[sel2, lns["tmp"]], writes=[pb], tag="l2b")
                        if dbg_tile:
                            ddump("dpb", pb[:, 0:TN], [pb])
                        fold = 8.0 if qk == 0 else 1.0
                        psw = ctx.ps[5]
                        eb.op("tensor", lambda e, i, mt=mt, t0=t0, psw=psw,
                              sheet=sheet: e.matmul(
                                  psw[:, 0:TN], pswap[:],
                                  sheet[:, mt * T + t0: mt * T + t0 + TN],
                                  start=True, stop=True),
                              reads=["w2p", "rawqk"], writes=[psw], tag="rsw")
                        eb.op("vector", lambda e, i, mt=mt, t0=t0, pb=pb,
                              sheet=sheet: e.tensor_mul(
                                  rt1[:],
                                  sheet[:, mt * T + t0: mt * T + t0 + TN],
                                  pb[:, 0:TN]),
                              reads=["rawqk", pb, xb], writes=[xb], tag="l2m")
                        eb.op("vector", lambda e, i, scol=scol, fold=fold:
                              e.tensor_scalar(rt1[:], rt1[:], scol[:, 0:1],
                                              fold, ALU.mult, ALU.mult),
                              reads=[xb, scol], writes=[xb], tag="l2s")
                        if dbg_tile:
                            ddump("drt1", rt1[:], [xb])
                        eb.op("vector", lambda e, i, psw=psw, sscol=sscol,
                              fold=fold: e.tensor_scalar(
                                  rt3[:], psw[:, 0:TN], sscol[:, 0:1],
                                  fold, ALU.mult, ALU.mult),
                              reads=[psw, sscol], writes=[xb], tag="rse")
                        eb.op("vector", lambda e, i, pb=pb: e.tensor_mul(
                            rt3[:], rt3[:], pb[:, 0:TN]),
                            reads=[xb, pb], writes=[xb], tag="rsm")
                        if dbg_tile:
                            ddump("drt3", rt3[:], [xb])
                        eb.op("vector", lambda e, i, t0=t0, cst=cst:
                              e.tensor_mul(rt2[:], rt1[:],
                                           cst[:, t0:t0 + TN]),
                              reads=[xb, "rotsheets"], writes=[xb],
                              tag="rc")
                        eb.op("vector", lambda e, i, t0=t0, sst=sst:
                              e.tensor_mul(sq[:], rt3[:],
                                           sst[:, t0:t0 + TN]),
                              reads=[xb, "rotsheets"], writes=[sq], tag="rs1")
                        eb.op("vector", lambda e, i: e.tensor_add(
                            rt2[:], rt2[:], sq[:]),
                            reads=[xb, sq], writes=[xb], tag="rs2")
                        if dbg_tile:
                            ddump("drt2", rt2[:], [xb])
                        eb.op("vector", lambda e, i, mt=mt, t0=t0, sheet=sheet:
                              e.tensor_copy(
                                  sheet[:, mt * T + t0: mt * T + t0 + TN],
                                  rt2[:]),
                              reads=[xb], writes=["rawqk"], tag="rw")

            if dbg and l == 0:
                dq = ctx.outp("dq", [128, 4 * T], BF16)
                dk = ctx.outp("dk", [128, 4 * T], BF16)
                dv = ctx.outp("dv", [128, 4 * T], BF16)
                eb.dma("sync", dq, qrot[:, :], reads=["rawqk"], tag="ddq")
                eb.dma("sync", dk, krot[:, :], reads=["rawqk"], tag="ddk")
                eb.dma("sync", dv, vsh[:, :], reads=["vsh"], tag="ddv")
                for nmn, tb in (("dcq", csQ), ("dsq", ssQ),
                                ("dck", csK), ("dsk", ssK)):
                    dout = ctx.outp(nmn, [128, T], BF16)
                    eb.dma("sync", dout, tb[:, :], reads=["rotsheets"],
                           tag="d" + nmn)

            for vt in range(4):
                for tt in range(16):
                    for h2 in range(2):
                        h = vt * 2 + h2
                        eb.op("tensor", lambda e, i, vt=vt, tt=tt, h2=h2:
                              e.transpose(
                                  ctx.pst[:, 0:64],
                                  vsh[h2 * 64:h2 * 64 + 64,
                                      vt * T + tt * 128: vt * T + tt * 128 + 128],
                                  ctx.identb[h2 * 64:h2 * 64 + 64,
                                             h2 * 64:h2 * 64 + 64]),
                              reads=["vsh", ctx.identb], writes=[ctx.pst],
                              tag="vTt")
                        eb.op("vector", lambda e, i, tt=tt, h=h: e.tensor_copy(
                            vT[:, tt * 520 + h * 65: tt * 520 + h * 65 + 64],
                            ctx.pst[:, 0:64]),
                            reads=[ctx.pst], writes=["vT"], tag="vTe")
            eb.op("vector", lambda e, i: e.memset(
                vT[:].rearrange("p (a h o) -> p a h o", a=16, h=8)[:, :, :, 64:65],
                1.0), writes=["vT"], tag="vT1")

            if l == 0 and not hasattr(ctx, "maskd_"):
                maskd = eb.sbuf("maskd", [128, 128], BF16)
                maskU = eb.sbuf("maskU", [128, 128], BF16)
                eb.op("gpsimd", lambda e, i, maskd=maskd: e.memset(maskd[:], 0.0),
                      writes=[maskd])
                eb.op("gpsimd", lambda e, i, maskd=maskd: e.affine_select(
                    out=maskd[:], in_=maskd[:], compare_op=ALU.is_ge,
                    fill=-1e30, base=0, pattern=[[-1, 128]],
                    channel_multiplier=1),
                    reads=[maskd], writes=[maskd])
                eb.op("tensor", lambda e, i, maskd=maskd: e.transpose(
                    ctx.pst[:], maskd[:], ctx.identb[:]),
                    reads=[maskd, ctx.identb], writes=[ctx.pst])
                eb.op("vector", lambda e, i, maskU=maskU: e.tensor_copy(
                    maskU[:], ctx.pst[:]), reads=[ctx.pst], writes=[maskU])
                ctx.maskd_ = maskd
                ctx.maskU_ = maskU
            maskd, maskU = ctx.maskd_, ctx.maskU_
            if dbg and l == 0:
                dmU = ctx.outp("dmU", [128, 128], F32)
                dmD = ctx.outp("dmD", [128, 128], F32)
                for nm_, msk_, dst_ in (("u", maskU, dmU), ("d", maskd, dmD)):
                    eb.op("vector", lambda e, i, msk_=msk_: e.tensor_copy(
                        lns["tmp"][:, 0:128], msk_[:]),
                        reads=[msk_], writes=[lns["tmp"]], tag="dmc" + nm_)
                    eb.dma("sync", dst_, lns["tmp"][:, 0:128],
                           reads=[lns["tmp"]], tag="dmd" + nm_)

            oblk = scr_bf16(17728, 256)
            rrec = lns["mu"][0:1, 128:256]
            ostg = lns["tmp"][0:64, 256:384]
            simb2 = SCR[:, 17984:18624]
            pbufA2 = scr_bf16(18624, 320)
            pTA2 = scr_bf16(18944, 320)
            oblk2 = scr_bf16(19264, 256)
            xcnk2 = SCR[:, 19520:20544]
            for it in range(16):
                oblkP = oblk if it % 2 == 0 else oblk2
                kOB = "oblk0" if it % 2 == 0 else "oblk1"
                xcnkP = xcnk if it % 2 == 0 else xcnk2
                kXC = "xc0" if it % 2 == 0 else "xc1"
                for h in range(H8):
                    par = (it * H8 + h) % 2
                    simbP = simb if par == 0 else simb2
                    kSIM = "simb" if par == 0 else "simbB"
                    pbufP = pbufA if par == 0 else pbufA2
                    kPB = "pb" if par == 0 else "pbB"
                    pTP = pTA if par == 0 else pTA2
                    kPT = "pT" if par == 0 else "pTB"
                    psAP = ctx.ps[4] if par == 0 else ctx.ps[5]
                    hb2 = (h % 2) * 64
                    hrow = h // 2
                    jlo = max(0, it - 4)
                    njt = it - jlo + 1
                    for jn, jt in enumerate(range(jlo, it + 1)):
                        ps = ctx.ps[jn % 3]
                        eb.op("tensor", lambda e, i, it=it, jt=jt, h=h, ps=ps,
                              hb2=hb2, hrow=hrow: e.matmul(
                                  ps[0:128, 0:128],
                                  qrot[hb2:hb2 + 64,
                                       hrow * T + it * 128: hrow * T + it * 128 + 128],
                                  krot[hb2:hb2 + 64,
                                       hrow * T + jt * 128: hrow * T + jt * 128 + 128],
                                  start=True, stop=True),
                              reads=["rawqk"], writes=[ps], tag="lsim")
                        msk = maskd if jt == it else (
                            maskU if jt == it - 4 else None)
                        if msk is not None:
                            eb.op("tensor", lambda e, i, ps=ps, msk=msk:
                                  e.matmul(ps[0:128, 0:128], ctx.identb[:],
                                           msk[:], start=False, stop=True,
                                           skip_group_check=True),
                                  reads=[ctx.identb, msk], writes=[ps],
                                  tag="lmsk")
                        eb.op("scalar", lambda e, i, jn=jn, ps=ps,
                              simbP=simbP, kSIM=kSIM: e.copy(
                            simbP[:, jn * 128:(jn + 1) * 128], ps[0:128, 0:128]),
                            reads=[ps], writes=[kSIM], tag="lsime")
                    jw = njt * 128
                    rmaxc = lns["sq"][0][:, 0:1]
                    eb.op("vector", lambda e, i, jw=jw, rmaxc=rmaxc,
                          simbP=simbP, kSIM=kSIM:
                          e.reduce_max(rmaxc, simbP[0:128, 0:jw], axis=AX.X,
                                       negate=True),
                          reads=[kSIM], writes=[lns["sq"][0]], tag="lrm")
                    eb.op("scalar", lambda e, i, jw=jw, rmaxc=rmaxc,
                          simbP=simbP, kSIM=kSIM, pbufP=pbufP, kPB=kPB:
                          e.activation(pbufP[:, 0:jw], simbP[:, 0:jw], AF.Exp,
                                       bias=rmaxc),
                          reads=[kSIM, lns["sq"][0]], writes=[kPB],
                          tag="lex")
                    if dbg and l == 0 and it == 8 and h == 0:
                        dsimb = ctx.outp("dsimb", [128, 640], F32)
                        eb.dma("sync", dsimb, simb[:, 0:640],
                               reads=["simb"], tag="ddsimb")
                        dpba = ctx.outp("dpba", [128, 640], BF16)
                        eb.dma("sync", dpba, pbufA[:, 0:640],
                               reads=["pb"], tag="ddpba")
                    for jn in range(njt):
                        pstP = ctx.pst if jn % 2 == 0 else ctx.pst2
                        kPS = ctx.pst if jn % 2 == 0 else ctx.psr
                        eb.op("tensor", lambda e, i, jn=jn, pbufP=pbufP,
                              kPB=kPB, pstP=pstP: e.transpose(
                            pstP[:], pbufP[:, jn * 128:(jn + 1) * 128],
                            ctx.identb[:]),
                            reads=[kPB, ctx.identb], writes=[kPS],
                            tag="lpT")
                        eb.op("vector", lambda e, i, jn=jn, pTP=pTP, kPT=kPT,
                              pstP=pstP: e.tensor_copy(
                            pTP[:, jn * 128:(jn + 1) * 128], pstP[:]),
                            reads=[kPS], writes=[kPT], tag="lpTe")
                    psA = psAP
                    for jn, jt in enumerate(range(jlo, it + 1)):
                        eb.op("tensor", lambda e, i, jn=jn, jt=jt, h=h, psA=psA,
                              njt=njt, pTP=pTP, kPT=kPT: e.matmul(
                                  psA[0:65, 0:128],
                                  vT[:, jt * 520 + h * 65: jt * 520 + h * 65 + 65],
                                  pTP[:, jn * 128:(jn + 1) * 128],
                                  start=(jn == 0), stop=(jn == njt - 1)),
                              reads=["vT", kPT], writes=[psA], tag="lav")
                    eb.op("vector", lambda e, i, psA=psA: e.reciprocal(
                        rrec, psA[64:65, 0:128]),
                        reads=[psA], writes=[lns["mu"]], tag="lrc")
                    psB = ctx.ps[3]
                    eb.op("tensor", lambda e, i, psB=psB: e.matmul(
                        psB[0:64, 0:128], ctx.ones1[0:1, 0:64], rrec,
                        start=True, stop=True),
                        reads=[ctx.ones1, lns["mu"]], writes=[psB], tag="lbc")
                    eb.op("scalar", lambda e, i, psA=psA: e.copy(
                        ostg, psA[0:64, 0:128]),
                        reads=[psA], writes=[lns["tmp"]], tag="loe")
                    eb.op("vector", lambda e, i, h=h, psB=psB, hb2=hb2,
                          hrow=hrow, oblkP=oblkP, kOB=kOB: e.tensor_mul(
                              oblkP[:, :][hb2:hb2 + 64,
                                          hrow * 128:(hrow + 1) * 128],
                              ostg, psB[0:64, 0:128]),
                          reads=[lns["tmp"], psB], writes=[kOB], tag="los")
                eb.dma("sync", xcnkP[:].rearrange("p (k t) -> p k t", k=KT)
                       if it % 2 == 0 else
                       xcnkP.rearrange("p (k t) -> p k t", k=KT),
                       vscr.rearrange("(k p) t -> p k t", p=128)
                       [:, :, it * 128:(it + 1) * 128],
                       reads=[vscr], writes=[kXC], tag="lxl")
                for m in range(KT):
                    ps = ctx.ps[m % 3]
                    for kd in range(4):
                        eb.op("tensor", lambda e, i, m=m, kd=kd, ps=ps,
                              oblkP=oblkP, kOB=kOB:
                              e.matmul(ps[0:128, 0:128],
                                       wo[:, kd * D + m * 128: kd * D + m * 128 + 128],
                                       oblkP[:, kd * 128:(kd + 1) * 128],
                                       start=(kd == 0), stop=(kd == 3)),
                              reads=[wo, kOB], writes=[ps], tag="lwo")
                    eb.op("vector", lambda e, i, m=m, ps=ps, xcnkP=xcnkP,
                          kXC=kXC: e.tensor_add(
                        xcnkP[:, m * 128:(m + 1) * 128],
                        xcnkP[:, m * 128:(m + 1) * 128], ps[0:128, 0:128]),
                        reads=[ps, kXC], writes=[kXC], tag="lres")
                eb.dma("sync", vscr.rearrange("(k p) t -> p k t", p=128)
                       [:, :, it * 128:(it + 1) * 128],
                       xcnkP[:].rearrange("p (k t) -> p k t", k=KT)
                       if it % 2 == 0 else
                       xcnkP.rearrange("p (k t) -> p k t", k=KT),
                       reads=[kXC], writes=[vscr], tag="lxs")
                upd_sheet(xcnkP, it * 128, 128, "lxu", rk=kXC)

            if dbg and l == 0:
                dsa = ctx.outp("dsa", [D, T], F32)
                for c in range(NB):
                    load_xblock(ctx, xb, vscr, c, tag="dsal", res=vscr)
                    store_xblock(ctx, xb, dsa, c, tag="dsas", res=None)
            eb.full_barrier(tag=f"fence_a{l}")
            lg, lb = sm[f"lg{l}"], sm[f"lb{l}"]
            b1a, b1g, b2 = sm[f"b1a{l}"], sm[f"b1g{l}"], sm[f"b2{l}"]
            eb.dma("gpsimd", w2p.rearrange("p (k m) -> p k m", k=W1M),
                   ih[f"w2p{l}"].rearrange("(k p) m -> p k m", p=128),
                   writes=["w2p"], tag=f"w2l{l}")
            xnb2 = wqkv[:, 0:KT * 1024]
            for half in range(2):
                th0 = half * (T // 2)
                for cc in range(2):
                    t0 = th0 + cc * TN
                    for k in range(KT):
                        off = GT + t0 - (1 if k >= 4 else 0)
                        eb.op("vector", lambda e, i, k=k, off=off, TP=TP: e.tensor_copy(
                            xb[:, k * TN:(k + 1) * TN],
                            xsheet[:, k * TP + off: k * TP + off + TN]),
                            reads=[xsheet], writes=[xb], tag="fsh")
                    ln_block(ctx, lns, xb, xb, 0, out_block=True)
                    for k in range(KT):
                        eb.op("vector", lambda e, i, k=k, cc=cc, lg=lg, lb=lb:
                              e.tensor_scalar(
                                  xnb2[:, k * 1024 + cc * TN:
                                       k * 1024 + cc * TN + TN],
                                  xb[:, k * TN:(k + 1) * TN],
                                  lg[:, k:k + 1], lb[:, k:k + 1],
                                  ALU.mult, ALU.add),
                              reads=[xb, lg, lb], writes=[wqkv],
                              tag="flgb")
                for m in range(W1M):
                    wa = w1a[m % 2]
                    wg = w1g[m % 2]
                    eb.dma("gpsimd",
                           wa[:].rearrange("p (k m) -> p k m", k=KT),
                           ih[f"w1p{l}"].rearrange("(k p) m -> p k m", p=128)
                           [:, :, m * 128:(m + 1) * 128],
                           writes=[wa], tag="w1a")
                    eb.dma("gpsimd",
                           wg[:].rearrange("p (k m) -> p k m", k=KT),
                           ih[f"w1p{l}"].rearrange("(k p) m -> p k m", p=128)
                           [:, :, (W1M + m) * 128:(W1M + m + 1) * 128],
                           writes=[wg], tag="w1g")
                    for cc in range(2):
                        idx = (m * 2 + cc) % 2
                        psa = ctx.ps[0 + 2 * idx]
                        psg = ctx.ps[1 + 2 * idx]
                        for kd in range(KT):
                            eb.op("tensor", lambda e, i, kd=kd, wa=wa, psa=psa,
                                  cc=cc: e.matmul(
                                      psa[:], wa[:, kd * 128: kd * 128 + 128],
                                      xnb2[:, kd * 1024 + cc * TN:
                                           kd * 1024 + cc * TN + TN],
                                      start=(kd == 0), stop=(kd == KT - 1)),
                                  reads=[wa, wqkv], writes=[psa],
                                  tag="f1a")
                            eb.op("tensor", lambda e, i, kd=kd, wg=wg, psg=psg,
                                  cc=cc: e.matmul(
                                      psg[:], wg[:, kd * 128: kd * 128 + 128],
                                      xnb2[:, kd * 1024 + cc * TN:
                                           kd * 1024 + cc * TN + TN],
                                      start=(kd == 0), stop=(kd == KT - 1)),
                                  reads=[wg, wqkv], writes=[psg],
                                  tag="f1g")
                        eb.op("scalar", lambda e, i, m=m, psg=psg, b1g=b1g:
                              e.activation(lns["tmp"][:], psg[:], AF.Gelu,
                                           bias=b1g[:, m:m + 1]),
                              reads=[psg, b1g], writes=[lns["tmp"]], tag="fge")
                        eb.op("vector", lambda e, i, m=m, cc=cc, psa=psa,
                              b1a=b1a: e.scalar_tensor_tensor(
                                  hbuf[:, m * 1024 + cc * TN:
                                       m * 1024 + cc * TN + TN],
                                  psa[:], b1a[:, m:m + 1], lns["tmp"][:],
                                  ALU.add, ALU.mult),
                              reads=[psa, b1a, lns["tmp"]], writes=["hbuf"],
                              tag="fhw")
                for cc in range(2):
                    t0 = th0 + cc * TN
                    c4 = t0 // TN
                    load_xblock(ctx, xb, vscr, c4, tag="fxl", res=vscr)
                    for m2 in range(KT):
                        ps = ctx.ps[4 + m2 % 2]
                        for kd in range(W1M):
                            eb.op("tensor", lambda e, i, m2=m2, kd=kd, ps=ps,
                                  cc=cc, W1M=W1M: e.matmul(
                                      ps[:],
                                      w2p[:, kd * D + m2 * 128:
                                          kd * D + m2 * 128 + 128],
                                      hbuf[:, kd * 1024 + cc * TN:
                                           kd * 1024 + cc * TN + TN],
                                      start=(kd == 0), stop=(kd == W1M - 1)),
                                  reads=["w2p", "hbuf"], writes=[ps], tag="f2")
                        eb.op("vector", lambda e, i, m2=m2, ps=ps, b2=b2:
                              e.scalar_tensor_tensor(
                                  xb[:, m2 * TN:(m2 + 1) * TN], ps[:],
                                  b2[:, m2:m2 + 1],
                                  xb[:, m2 * TN:(m2 + 1) * TN],
                                  ALU.add, ALU.add),
                              reads=[ps, b2, xb], writes=[xb], tag="f2r")
                    store_xblock(ctx, xb, vscr, c4, tag="fxs", res=vscr)
                    upd_sheet(xb, c4 * TN, TN, "fxu")
            if dbg and l == 0:
                dsf = ctx.outp("dsf", [D, T], F32)
                for c in range(NB):
                    load_xblock(ctx, xb, vscr, c, tag="dsfl", res=vscr)
                    store_xblock(ctx, xb, dsf, c, tag="dsfs", res=None)
            eb.full_barrier(tag=f"fence_f{l}")
        tmpf = lns["tmp"]
        xv2 = x.rearrange("(k p) t -> p k t", p=128)
        for c in range(NB):
            load_xblock(ctx, xb, vscr, c, tag="ffin", res=vscr)
            for k in range(KT):
                eb.dma("sync", tmpf[:], xv2[:, k, c * TN:(c + 1) * TN],
                       writes=[tmpf], tag="fxf")
                eb.op("vector", lambda e, i, k=k: e.tensor_add(
                    xb[:, k * TN:(k + 1) * TN], xb[:, k * TN:(k + 1) * TN],
                    tmpf[:]), reads=[xb, tmpf], writes=[xb])
            store_xblock(ctx, xb, xo, c)
        eb.barrier("sync")
        eb.finish()
    return ctx.nc



TLRW = TN
NTLRW = T // TLRW
GRW = 16


def build_rwkv(stage=3, dbg=False):
    PS5 = ["r", "k", "v", "g", "d"]
    with Ctx("rwkv") as ctx:
        eb = ctx.eb
        nc = ctx.nc
        x = ctx.inp("x", [D, T])
        ih = {}
        for nm in ["rwT", "kwT", "vwT", "gwT", "owT"]:
            ih[nm] = ctx.inp(nm, [D, D])
        for p in PS5:
            ih[f"la{p}"] = ctx.inp(f"la{p}", [D, 64])
            ih[f"lb{p}"] = ctx.inp(f"lb{p}", [64, D])
            ih[f"ll{p}"] = ctx.inp(f"ll{p}", [128, KT])
        ih["xwc"] = ctx.inp("xwc", [128, KT])
        ih["uc"] = ctx.inp("uc", [128, KT])
        ih["lnw"] = ctx.inp("lnw", [128, 1])
        ih["lnbc"] = ctx.inp("lnbc", [128, 1])
        ih["sinit"] = ctx.inp("sinit", [128, 512])
        xo = ctx.outp("xo", [D, T])

        wR = load_weight(ctx, "wRt", ih["rwT"], D, D)
        wK = load_weight(ctx, "wKt", ih["kwT"], D, D)
        wV = load_weight(ctx, "wVt", ih["vwT"], D, D)
        wG = load_weight(ctx, "wGt", ih["gwT"], D, D)
        wO = load_weight(ctx, "wOt", ih["owT"], D, D)
        laB, lbB, llc = {}, {}, {}
        for p in PS5:
            laB[p] = load_weight(ctx, f"laB{p}", ih[f"la{p}"], D, 64)
            lb = eb.sbuf(f"lbB{p}", [64, D], BF16)
            eb.dma("gpsimd", lb[:], ih[f"lb{p}"], writes=[lb], tag=f"lb{p}")
            lbB[p] = lb
            llc[p] = eb.sbuf(f"llc{p}", [128, KT], F32)
            eb.dma("sync", llc[p][:], ih[f"ll{p}"], writes=[llc[p]])
        smalls = {}
        for nm, shp in (("xwc", [128, KT]), ("uc", [128, KT]),
                        ("lnw", [128, 1]), ("lnbc", [128, 1])):
            smalls[nm] = eb.sbuf(nm + "_s", shp, F32)
            eb.dma("sync", smalls[nm][:], ih[nm], writes=[smalls[nm]])
        xwc, uc = smalls["xwc"], smalls["uc"]
        lnw, lnbc = smalls["lnw"], smalls["lnbc"]

        xb = eb.sbuf("xb", [128, KT * TLRW], F32)
        wm = xb[:, 0:528]
        scA = xb[:, 528:1040]
        scB = xb[:, 1040:1552]
        scC = xb[:, 1552:2064]
        Qb = xb[:, 2064:2576]
        Yi = xb[:, 2576:3088]
        wg = xb[:, 2576:3089]
        rows = xb[:, 3089:3601]
        pccol = xb[:, 3601:3602]
        xnb = eb.sbuf("xnb", [128, KT * (TLRW + 1)], BF16)
        dxb = eb.sbuf("dxb", [128, KT * TLRW], BF16)
        osheet = dxb
        lerpb = eb.sbuf("lerpb", [128, KT * TLRW], BF16)
        xres = lerpb[:, 0:1024].bitcast(F32)
        mixb = eb.sbuf("mixb", [128, KT * TLRW], BF16)
        mixf = mixb[:, :].bitcast(F32)
        lns = dict(tmp=mixf[:, 0:512], sq=[None, None],
                   mu=mixf[0:1, 1024:1536], ms=mixf[0:1, 1536:2048])
        _s0 = mixf[:, 512:1024]
        lns["sq"] = [_s0, _s0]
        r16b = mixb[:, 0:512]
        r32b = mixb[:, 512:1024]
        k16r = mixb[:, 1024:1536]
        k32r = mixb[:, 1536:2048]
        vrevb = mixb[:, 2048:2560]
        kT16 = mixb[:, 2560:3072]
        kT32 = mixb[:, 3072:3584]
        vTb = mixb[:, 3584:4096]
        rsh = eb.sbuf("rsh", [128, KT * TLRW], BF16)
        gsh = eb.sbuf("gsh", [128, KT * TLRW], BF16)
        ksh = eb.sbuf("ksh", [128, KT * (GRW + TLRW)], BF16)
        vsh = eb.sbuf("vsh", [128, KT * (GRW + TLRW)], BF16)
        wsh = eb.sbuf("wsh", [128, KT * TLRW], F32)
        tmpb = eb.sbuf("tmpb", [128, TLRW], BF16)
        ind16 = eb.sbuf("ind16", [128, TLRW], F32)
        ind32 = eb.sbuf("ind32", [128, TLRW], F32)
        mask16 = eb.sbuf("mask16", [128, TLRW], BF16)
        mask32 = eb.sbuf("mask32", [128, TLRW], BF16)
        Ssheet = eb.sbuf("Ssheet", [128, 512], F32)
        Sb = eb.sbuf("Sbb", [128, 64], BF16)
        sgm = eb.sbuf("sgm", [128, 64], BF16)
        xcar = eb.sbuf("xcar", [128, KT], F32)

        for ind, seg in ((ind16, 16), (ind32, 32)):
            eb.op("gpsimd", lambda e, i, ind=ind, seg=seg: e.iota(
                ind[:], pattern=[[0, TLRW // seg], [1, seg]], base=0,
                channel_multiplier=0, allow_small_or_imprecise_dtypes=True),
                writes=[ind], tag="iio")
            eb.op("vector", lambda e, i, ind=ind: e.tensor_single_scalar(
                ind[:], ind[:], 0.5, ALU.is_lt),
                reads=[ind], writes=[ind], tag="iic")
        for msk, ind in ((mask16, ind16), (mask32, ind32)):
            eb.op("vector", lambda e, i, msk=msk, ind=ind: e.tensor_scalar(
                msk[:], ind[:], -1.0, 1.0, ALU.mult, ALU.add),
                reads=[ind], writes=[msk], tag="imk")
        ones2 = eb.sbuf("ones2r", [128, 2], F32)
        eb.op("vector", lambda e, i: e.memset(ones2[:], 0.0), writes=[ones2])
        eb.op("vector", lambda e, i: e.memset(ones2[0:64, 0:1], 1.0),
              reads=[ones2], writes=[ones2])
        eb.op("vector", lambda e, i: e.memset(ones2[64:128, 1:2], 1.0),
              reads=[ones2], writes=[ones2])
        onesb2 = eb.sbuf("onesb2r", [128, 2], BF16)
        eb.op("vector", lambda e, i: e.tensor_copy(onesb2[:], ones2[:]),
              reads=[ones2], writes=[onesb2])
        sel2 = eb.sbuf("sel2r", [2, 128], F32)
        eb.op("tensor", lambda e, i: e.matmul(
            ctx.psr[0:2, 0:128], onesb2[:], ctx.identb[:],
            start=True, stop=True), reads=[onesb2, ctx.identb],
            writes=[ctx.psr])
        eb.op("vector", lambda e, i: e.tensor_copy(
            sel2[:], ctx.psr[0:2, 0:128]), reads=[ctx.psr], writes=[sel2])
        sel2b = eb.sbuf("sel2b", [2, 128], BF16)
        eb.op("vector", lambda e, i: e.tensor_copy(sel2b[:], sel2[:]),
              reads=[sel2], writes=[sel2b])
        rowsb = eb.sbuf("rowsb", [2, TLRW], BF16)
        for sh in (ksh, vsh):
            for k in range(KT):
                eb.op("gpsimd", lambda e, i, sh=sh, k=k, GRW=GRW, TLRW=TLRW: e.memset(
                    sh[:, k * (GRW + TLRW): k * (GRW + TLRW) + GRW], 0.0),
                    writes=[sh], tag="gz")
        eb.op("vector", lambda e, i: e.memset(xcar[:], 0.0), writes=[xcar])
        eb.dma("sync", Ssheet[:], ih["sinit"], writes=["Ss"], tag="ldS")

        def blk(sheet, m, width=TLRW, off=0):
            return sheet[:, m * width + off: m * width + off + TLRW]

        def kvblk(sheet, m, sh=0):
            return sheet[:, m * (GRW + TLRW) + GRW - sh: m * (GRW + TLRW) + GRW - sh + TLRW]

        for tc in range(NTLRW):
            t0 = tc * TLRW
            load_xblock(ctx, xb, x, tc, tag="xl")
            if stage < 3:
                store_xblock(ctx, xb, xo, tc, tag="xdum")
            for k in range(KT):
                eb.op("vector", lambda e, i, k=k, TLRW=TLRW: e.tensor_copy(
                    xnb[:, k * (TLRW + 1): k * (TLRW + 1) + 1], xcar[:, k:k + 1]),
                    reads=[xcar], writes=[xnb], tag="xg")
            ln_block(ctx, lns, xb, lerpb, 0, tag="l0", out_block=True)
            for k in range(KT):
                eb.op("vector", lambda e, i, k=k: e.tensor_copy(
                    xnb[:, k * (TLRW + 1) + 1: k * (TLRW + 1) + 1 + TLRW],
                    blk(lerpb, k)),
                    reads=[lerpb], writes=[xnb], tag="xnc")
            for k in range(KT):
                eb.op("vector", lambda e, i, k=k, TLRW=TLRW: e.tensor_copy(
                    xcar[:, k:k + 1], xnb[:, k * (TLRW + 1) + TLRW: k * (TLRW + 1) + TLRW + 1]),
                    reads=[xnb], writes=[xcar], tag="xc")
            for k in range(KT):
                eb.op("vector", lambda e, i, k=k, TLRW=TLRW: e.tensor_sub(
                    blk(dxb, k), xnb[:, k * (TLRW + 1): k * (TLRW + 1) + TLRW],
                    xnb[:, k * (TLRW + 1) + 1: k * (TLRW + 1) + 1 + TLRW]),
                    reads=[xnb], writes=[dxb], tag="dx")
                eb.op("vector", lambda e, i, k=k: e.tensor_scalar_mul(
                    scB, blk(dxb, k), xwc[:, k:k + 1]),
                    reads=[dxb, xwc], writes=["scB"], tag="lp1")
                eb.op("vector", lambda e, i, k=k, TLRW=TLRW: e.tensor_add(
                    blk(lerpb, k), scB,
                    xnb[:, k * (TLRW + 1) + 1: k * (TLRW + 1) + 1 + TLRW]),
                    reads=["scB", xnb], writes=[lerpb], tag="lp2")

            def lora_pass(p, rhs_sheet, rhs_key):
                psA = ctx.ps[0]
                for kd in range(KT):
                    eb.op("tensor", lambda e, i, p=p, kd=kd, psA=psA,
                          rhs_sheet=rhs_sheet: e.matmul(
                              psA[0:64, 0:TLRW], laB[p][:, kd * 64:(kd + 1) * 64],
                              blk(rhs_sheet, kd),
                              start=(kd == 0), stop=(kd == KT - 1)),
                          reads=[laB[p], rhs_key], writes=[psA], tag="lra")
                eb.op("scalar", lambda e, i, psA=psA: e.activation(
                    thb[:], psA[0:64, 0:TLRW], AF.Sigmoid, scale=2.0),
                    reads=[psA], writes=[thb], tag="lrt")
                eb.op("vector", lambda e, i: e.tensor_scalar(
                    thb[:], thb[:], 2.0, -1.0, ALU.mult, ALU.add),
                    reads=[thb], writes=[thb], tag="lrt2")

            thb = (eb.sbuf("thb", [64, TLRW], BF16) if tc == 0
                   else thb)
            for p in PS5:
                lora_pass(p, lerpb, lerpb)
                for mo in range(KT):
                    ps = ctx.ps[1 + mo % 2]
                    eb.op("tensor", lambda e, i, p=p, mo=mo, ps=ps: e.matmul(
                        ps[:], lbB[p][:, mo * 128:(mo + 1) * 128], thb[:],
                        start=True, stop=True),
                        reads=[lbB[p], thb], writes=[ps], tag="lrb")
                    eb.op("vector", lambda e, i, p=p, mo=mo, ps=ps:
                          e.tensor_scalar_add(scB, ps[:], llc[p][:, mo:mo + 1]),
                          reads=[ps, llc[p]], writes=["scB"], tag="mx1")
                    eb.op("vector", lambda e, i, mo=mo: e.tensor_mul(
                        scB, scB, blk(dxb, mo)),
                        reads=["scB", dxb], writes=["scB"], tag="mx2")
                    eb.op("vector", lambda e, i, mo=mo, TLRW=TLRW: e.tensor_add(
                        blk(mixb, mo), scB,
                        xnb[:, mo * (TLRW + 1) + 1: mo * (TLRW + 1) + 1 + TLRW]),
                        reads=["scB", xnb], writes=["mixu"], tag="mx3")
                if p == "d":
                    lora_pass("d", mixb, "mixu")
                    for mo in range(KT):
                        ps = ctx.ps[1 + mo % 2]
                        eb.op("tensor", lambda e, i, mo=mo, ps=ps: e.matmul(
                            ps[:], lbB["d"][:, mo * 128:(mo + 1) * 128], thb[:],
                            start=True, stop=True),
                            reads=[lbB["d"], thb], writes=[ps], tag="lrb2")
                        eb.op("scalar", lambda e, i, mo=mo, ps=ps:
                              e.activation(scB, ps[:], AF.Exp,
                                           bias=llc["d"][:, mo:mo + 1]),
                              reads=[ps, llc["d"]], writes=["scB"], tag="wx1")
                        eb.op("scalar", lambda e, i, mo=mo: e.activation(
                            blk(wsh, mo), scB, AF.Exp, scale=-1.0),
                            reads=["scB"], writes=[wsh], tag="wx2")
                else:
                    wW = dict(r=wR, k=wK, v=wV, g=wG)[p]
                    dst = dict(r=rsh, k=ksh, v=vsh, g=gsh)[p]
                    for mo in range(KT):
                        ps = ctx.ps[2 + mo % 2]
                        for kd in range(KT):
                            eb.op("tensor", lambda e, i, p=p, mo=mo, kd=kd,
                                  ps=ps, wW=wW: e.matmul(
                                      ps[:], wslice(wW, D, kd, mo * 128),
                                      blk(mixb, kd),
                                      start=(kd == 0), stop=(kd == KT - 1)),
                                  reads=[wW, "mixu"], writes=[ps], tag="pj")
                        if p in ("k", "v"):
                            eb.op("scalar", lambda e, i, mo=mo, ps=ps,
                                  dst=dst, GRW=GRW, TLRW=TLRW: e.copy(
                                      dst[:, mo * (GRW + TLRW) + GRW:
                                          mo * (GRW + TLRW) + GRW + TLRW], ps[:]),
                                  reads=[ps], writes=[dst], tag="pje")
                        else:
                            eb.op("scalar", lambda e, i, mo=mo, ps=ps, dst=dst:
                                  e.copy(blk(dst, mo), ps[:]),
                                  reads=[ps], writes=[dst], tag="pje")

            if dbg and tc == 0:
                for nm, sh in (("dw", wsh), ("dr", rsh), ("dk", ksh),
                               ("dv", vsh), ("dg", gsh)):
                    dt_ = F32 if nm == "dw" else BF16
                    dout = ctx.outp(nm, [128, sh.shape[1]], dt_)
                    eb.dma("sync", dout, sh[:, :], reads=[sh], tag="d" + nm)

            eb.full_barrier(tag=f"fb_a{tc}")
            if stage < 2:
                eb.full_barrier(tag=f"fb_b{tc}")
                continue

            for m in range(KT):
                dbm = dbg and tc == 0 and m == 0
                eb.op("gpsimd", lambda e, i: e.memset(wg[:, 0:1], 0.0),
                      reads=["Yi"], writes=["Yi"], tag="wg0")
                eb.op("gpsimd", lambda e, i, m=m: e.tensor_copy(
                    wg[:, 1:513], blk(wsh, m)),
                    reads=[wsh, "Yi"], writes=["Yi"], tag="wgc")
                eb.op("gpsimd", lambda e, i: e.memset(wm[:, 0:GRW], 0.0),
                      reads=["wm"], writes=["wm"], tag="wmg")
                eb.op("vector", lambda e, i, m=m: e.tensor_mul(
                    wm[:, GRW:GRW + TLRW], blk(wsh, m), mask16[:]),
                    reads=[wsh, mask16, "wm"], writes=["wm"], tag="wmk")
                eb.op("vector", lambda e, i: e.tensor_mul(
                    scA, wg[:, 0:TLRW], mask16[:]),
                    reads=["Yi", mask16], writes=["scA"], tag="p16d")
                eb.op("vector", lambda e, i: e.tensor_tensor_scan(
                    scB, scA, ind16[:], 0.0, ALU.mult, ALU.add),
                    reads=["scA", ind16], writes=["scB"], tag="p16s")
                eb.op("vector", lambda e, i, m=m: e.tensor_mul(
                    r16b, blk(rsh, m), scB),
                    reads=[rsh, "scB"], writes=["mixu"], tag="r16")
                eb.op("vector", lambda e, i: e.tensor_mul(
                    scA, wg[:, 0:TLRW], mask32[:]),
                    reads=["Yi", mask32], writes=["scA"], tag="p32d")
                eb.op("vector", lambda e, i: e.tensor_tensor_scan(
                    scC, scA, ind32[:], 0.0, ALU.mult, ALU.add),
                    reads=["scA", ind32], writes=["scC"], tag="p32s")
                eb.op("vector", lambda e, i, m=m: e.tensor_mul(
                    r32b, blk(rsh, m), scC),
                    reads=[rsh, "scC"], writes=["mixu"], tag="r32")
                if dbm:
                    for nm, src in (("dP16", "scB-"), ("dP32", "scC-")):
                        dout = ctx.outp(nm, [128, TLRW], F32)
                        eb.dma("sync", dout, scB if nm == "dP16" else scC,
                               reads=[nm[1:4].replace("P1", "scB").replace(
                                   "P3", "scC")] if False else
                               ["scB" if nm == "dP16" else "scC"],
                               tag="dd" + nm)
                eb.op("vector", lambda e, i, m=m: e.tensor_mul(
                    scA, scB, blk(wsh, m)),
                    reads=["scB", wsh], writes=["scA"], tag="p16i")
                eb.op("gpsimd", lambda e, i, m=m: e.tensor_copy(
                    wg[:, 1:513].rearrange("p (s j) -> p s j", j=16),
                    blk(wsh, m).rearrange("p (s j) -> p s j", j=16)[:, :, ::-1]),
                    reads=[wsh, "Yi"], writes=["Yi"], tag="wrv")
                eb.op("vector", lambda e, i: e.tensor_mul(
                    scB, wg[:, 0:TLRW], mask16[:]),
                    reads=["Yi", mask16], writes=["scB"], tag="k16d")
                eb.op("vector", lambda e, i: e.tensor_tensor_scan(
                    Qb, scB, ind16[:], 0.0, ALU.mult, ALU.add),
                    reads=["scB", ind16], writes=["Qb"], tag="k16s")
                if dbm:
                    dout = ctx.outp("dK16", [128, TLRW], F32)
                    eb.dma("sync", dout, Qb, reads=["Qb"], tag="ddK16")
                eb.op("vector", lambda e, i, m=m: e.tensor_mul(
                    k16r.rearrange("p (s j) -> p s j", j=16),
                    kvblk(ksh, m).rearrange("p (s j) -> p s j", j=16)[:, :, ::-1],
                    Qb.rearrange("p (s j) -> p s j", j=16)),
                    reads=[ksh, "Qb"], writes=["mixu"], tag="k16r")
                eb.op("gpsimd", lambda e, i, m=m: e.tensor_copy(
                    vrevb.rearrange("p (s j) -> p s j", j=16),
                    kvblk(vsh, m).rearrange("p (s j) -> p s j", j=16)[:, :, ::-1]),
                    reads=[vsh], writes=["mixu"], tag="vrv")
                eb.op("gpsimd", lambda e, i: e.tensor_copy(k32r, k16r),
                      reads=["mixu"], writes=["mixu"], tag="k32c")
                for c in range(16):
                    eb.op("vector", lambda e, i, c=c: e.tensor_scalar_mul(
                        k32r[:, 32 * c:32 * c + 16],
                        k32r[:, 32 * c:32 * c + 16],
                        scA[:, 32 * c + 31:32 * c + 32]),
                        reads=["mixu", "scA"], writes=["mixu"], tag="k32p")
                for b in range(4):
                    for si, (src, dst) in enumerate(((k16r, kT16),
                                                    (k32r, kT32),
                                                    (vrevb, vTb))):
                        pstP = ctx.pst if (b * 3 + si) % 2 == 0 else ctx.pst2
                        kPS = (ctx.pst if (b * 3 + si) % 2 == 0
                               else ctx.psr)
                        eb.op("tensor", lambda e, i, b=b, src=src, pstP=pstP:
                              e.transpose(
                            pstP[:], src[:, b * 128:(b + 1) * 128],
                            ctx.identb[:]),
                            reads=["mixu", ctx.identb], writes=[kPS],
                            tag="tr")
                        eb.op("vector", lambda e, i, b=b, dst=dst, pstP=pstP:
                              e.tensor_copy(dst[:, b * 128:(b + 1) * 128],
                                            pstP[:]),
                              reads=[kPS], writes=["mixu"], tag="tre")
                psY = ctx.ps[4]
                psS = ctx.ps[1]
                for c in range(16):
                    b = c // 4
                    rl = 32 * (c % 4)
                    eb.op("scalar", lambda e, i, m=m: e.copy(
                        Sb[:], Ssheet[:, m * 64:(m + 1) * 64]),
                        reads=["Ss"], writes=[Sb], tag="sbc")
                    for h2 in range(2):
                        eb.op("tensor", lambda e, i, b=b, rl=rl, h2=h2,
                              psS=psS: e.matmul(
                                  psS[h2 * 64:(h2 + 1) * 64, 0:64],
                                  kT16[rl:rl + 16,
                                       b * 128 + h2 * 64: b * 128 + h2 * 64 + 64],
                                  vTb[rl:rl + 16,
                                      b * 128 + h2 * 64: b * 128 + h2 * 64 + 64],
                                  start=True, stop=True,
                                  tile_position=(rl, h2 * 64)),
                              reads=["mixu"], writes=[psS], tag="sgA")
                    eb.op("scalar", lambda e, i, psS=psS: e.copy(
                        sgm[:], psS[:, 0:64]),
                        reads=[psS], writes=[sgm], tag="sge")
                    for h2 in range(2):
                        eb.op("tensor", lambda e, i, c=c, h2=h2, psY=psY:
                              e.matmul(
                                  psY[h2 * 64:(h2 + 1) * 64, 32 * c:32 * c + 16],
                                  Sb[h2 * 64:(h2 + 1) * 64, 0:64],
                                  r32b[h2 * 64:(h2 + 1) * 64, 32 * c:32 * c + 16],
                                  start=True, stop=True,
                                  tile_position=(h2 * 64, h2 * 64)),
                              reads=[Sb, "mixu"], writes=[psY], tag="ycA")
                        eb.op("tensor", lambda e, i, c=c, h2=h2, psY=psY:
                              e.matmul(
                                  psY[h2 * 64:(h2 + 1) * 64,
                                      32 * c + 16:32 * c + 32],
                                  Sb[h2 * 64:(h2 + 1) * 64, 0:64],
                                  r32b[h2 * 64:(h2 + 1) * 64,
                                       32 * c + 16:32 * c + 32],
                                  start=True, stop=False,
                                  tile_position=(h2 * 64, h2 * 64)),
                              reads=[Sb, "mixu"], writes=[psY], tag="ycB")
                        eb.op("tensor", lambda e, i, c=c, h2=h2, psY=psY:
                              e.matmul(
                                  psY[h2 * 64:(h2 + 1) * 64,
                                      32 * c + 16:32 * c + 32],
                                  sgm[h2 * 64:(h2 + 1) * 64, 0:64],
                                  r16b[h2 * 64:(h2 + 1) * 64,
                                       32 * c + 16:32 * c + 32],
                                  start=False, stop=True,
                                  tile_position=(h2 * 64, h2 * 64)),
                              reads=[sgm, "mixu"], writes=[psY], tag="ysg")
                        eb.op("tensor", lambda e, i, b=b, rl=rl, h2=h2,
                              psS=psS: e.matmul(
                                  psS[h2 * 64:(h2 + 1) * 64, 64:128],
                                  kT32[rl:rl + 32,
                                       b * 128 + h2 * 64: b * 128 + h2 * 64 + 64],
                                  vTb[rl:rl + 32,
                                      b * 128 + h2 * 64: b * 128 + h2 * 64 + 64],
                                  start=True, stop=True,
                                  tile_position=(rl, h2 * 64)),
                              reads=["mixu"], writes=[psS], tag="ssm")
                    eb.op("vector", lambda e, i, c=c, m=m: e.tensor_mul(
                        pccol, scC[:, 32 * c + 31:32 * c + 32],
                        blk(wsh, m)[:, 32 * c + 31:32 * c + 32]),
                        reads=["scC", wsh], writes=["rows"], tag="pcc")
                    eb.op("vector", lambda e, i, m=m: e.tensor_scalar_mul(
                        Ssheet[:, m * 64:(m + 1) * 64],
                        Ssheet[:, m * 64:(m + 1) * 64], pccol),
                        reads=["Ss", "rows"], writes=["Ss"], tag="sdc")
                    eb.op("vector", lambda e, i, m=m, psS=psS: e.tensor_add(
                        Ssheet[:, m * 64:(m + 1) * 64],
                        Ssheet[:, m * 64:(m + 1) * 64], psS[:, 64:128]),
                        reads=["Ss", psS], writes=["Ss"], tag="sda")
                    eb.op("vector", lambda e, i, c=c: e.tensor_scalar_mul(
                        scB[:, 0:64], sgm[:], scA[:, 32 * c + 31:32 * c + 32]),
                        reads=[sgm, "scA"], writes=["scB"], tag="sdp")
                    eb.op("vector", lambda e, i, m=m: e.tensor_add(
                        Ssheet[:, m * 64:(m + 1) * 64],
                        Ssheet[:, m * 64:(m + 1) * 64], scB[:, 0:64]),
                        reads=["Ss", "scB"], writes=["Ss"], tag="sdb")
                eb.op("vector", lambda e, i, m=m: e.tensor_scalar_mul(
                    Qb, blk(rsh, m), uc[:, m:m + 1]),
                    reads=[rsh, uc], writes=["Qb"], tag="qu")
                eb.op("vector", lambda e, i, m=m: e.tensor_mul(
                    tmpb[:], Qb, kvblk(ksh, m)),
                    reads=["Qb", ksh], writes=[tmpb], tag="t0")
                eb.op("tensor", lambda e, i: e.matmul(
                    ctx.psr[0:2, 0:TLRW], onesb2[:], tmpb[:],
                    start=True, stop=True),
                    reads=[onesb2, tmpb], writes=[ctx.psr], tag="dt0")
                eb.op("scalar", lambda e, i: e.copy(
                    rowsb[:], ctx.psr[0:2, 0:TLRW]),
                    reads=[ctx.psr], writes=[rowsb], tag="de0")
                eb.op("tensor", lambda e, i: e.matmul(
                    ctx.ps[0][:, 0:TLRW], sel2b[:], rowsb[:],
                    start=True, stop=True),
                    reads=[sel2b, rowsb], writes=[ctx.ps[0]], tag="db0")
                eb.op("vector", lambda e, i, m=m: e.tensor_mul(
                    Yi, ctx.ps[0][:, 0:TLRW], kvblk(vsh, m)),
                    reads=[ctx.ps[0], vsh], writes=["Yi"], tag="y0")
                eb.op("vector", lambda e, i, m=m: e.tensor_mul(
                    Qb, blk(rsh, m), mask16[:]),
                    reads=[rsh, mask16], writes=["Qb"], tag="q1")
                for l in range(1, 16):
                    if l > 1:
                        eb.op("vector", lambda e, i, l=l, GRW=GRW: e.tensor_mul(
                            Qb, Qb, wm[:, GRW - (l - 1): GRW - (l - 1) + TLRW]),
                            reads=["Qb", "wm"], writes=["Qb"], tag="ql")
                    eb.op("vector", lambda e, i, m=m, l=l: e.tensor_mul(
                        tmpb[:], Qb, kvblk(ksh, m, sh=l)),
                        reads=["Qb", ksh], writes=[tmpb], tag="tl")
                    eb.op("tensor", lambda e, i: e.matmul(
                        ctx.psr[0:2, 0:TLRW], onesb2[:], tmpb[:],
                        start=True, stop=True),
                        reads=[onesb2, tmpb], writes=[ctx.psr], tag="dtl")
                    eb.op("scalar", lambda e, i: e.copy(
                        rowsb[:], ctx.psr[0:2, 0:TLRW]),
                        reads=[ctx.psr], writes=[rowsb], tag="del")
                    eb.op("tensor", lambda e, i: e.matmul(
                        ctx.ps[0][:, 0:TLRW], sel2b[:], rowsb[:],
                        start=True, stop=True),
                        reads=[sel2b, rowsb], writes=[ctx.ps[0]], tag="dbl")
                    eb.op("vector", lambda e, i, m=m, l=l: e.tensor_mul(
                        scB, ctx.ps[0][:, 0:TLRW], kvblk(vsh, m, sh=l)),
                        reads=[ctx.ps[0], vsh], writes=["scB"], tag="yl")
                    eb.op("vector", lambda e, i: e.tensor_add(Yi, Yi, scB),
                          reads=["Yi", "scB"], writes=["Yi"], tag="ya")
                if dbm:
                    dout = ctx.outp("dYin", [128, TLRW], F32)
                    eb.dma("sync", dout, Yi, reads=["Yi"], tag="ddYin")
                    dyc = ctx.outp("dYc", [128, TLRW], F32)
                    eb.op("vector", lambda e, i, psY=psY: e.tensor_copy(
                        scB, psY[:, 0:TLRW]), reads=[psY], writes=["scB"],
                        tag="dyce")
                    eb.dma("sync", dyc, scB, reads=["scB"], tag="ddYc")
                eb.op("vector", lambda e, i, psY=psY: e.tensor_add(
                    Yi, Yi, psY[:, 0:TLRW]),
                    reads=["Yi", psY], writes=["Yi"], tag="yc")
                eb.op("tensor", lambda e, i: e.matmul(
                    ctx.psr[0:2, 0:TLRW], ones2[:], Yi, start=True, stop=True),
                    reads=[ones2, "Yi"], writes=[ctx.psr], tag="hs1")
                eb.op("scalar", lambda e, i: e.activation(
                    scC, Yi, AF.Square), reads=["Yi"], writes=["scC"],
                    tag="hsq")
                eb.op("tensor", lambda e, i: e.matmul(
                    ctx.ps[2][0:2, 0:TLRW], ones2[:], scC,
                    start=True, stop=True),
                    reads=[ones2, "scC"], writes=[ctx.ps[2]], tag="hs2")
                eb.op("vector", lambda e, i: e.tensor_scalar_mul(
                    rows[0:2, :], ctx.psr[0:2, 0:TLRW], 1.0 / 64),
                    reads=[ctx.psr], writes=["rows"], tag="hmu")
                eb.op("vector", lambda e, i: e.tensor_mul(
                    scC[0:2, :], rows[0:2, :], rows[0:2, :]),
                    reads=["rows"], writes=["scC"], tag="hm2")
                eb.op("vector", lambda e, i: e.scalar_tensor_tensor(
                    scB[0:2, :], ctx.ps[2][0:2, 0:TLRW], 1.0 / 64, scC[0:2, :],
                    ALU.mult, ALU.subtract),
                    reads=[ctx.ps[2], "scC"], writes=["scB"], tag="hvr")
                ep2 = ctx.constcol(1e-5, 2)
                eb.op("scalar", lambda e, i, b=ep2: e.activation(
                    scB[0:2, :], scB[0:2, :], AF.Ln, bias=b),
                    reads=["scB"], writes=["scB"], tag="hln")
                eb.op("scalar", lambda e, i: e.activation(
                    scB[0:2, :], scB[0:2, :], AF.Exp, scale=-0.5),
                    reads=["scB"], writes=["scB"], tag="hex")
                eb.op("tensor", lambda e, i: e.matmul(
                    ctx.ps[0][:, 0:TLRW], sel2[:], rows[0:2, :],
                    start=True, stop=True),
                    reads=[sel2, "rows"], writes=[ctx.ps[0]], tag="hbm")
                eb.op("tensor", lambda e, i: e.matmul(
                    ctx.ps[2][:, 0:TLRW], sel2[:], scB[0:2, :],
                    start=True, stop=True),
                    reads=[sel2, "scB"], writes=[ctx.ps[2]], tag="hbr")
                eb.op("vector", lambda e, i: e.tensor_sub(
                    scC, Yi, ctx.ps[0][:, 0:TLRW]),
                    reads=["Yi", ctx.ps[0]], writes=["scC"], tag="hn1")
                eb.op("vector", lambda e, i: e.tensor_mul(
                    scC, scC, ctx.ps[2][:, 0:TLRW]),
                    reads=["scC", ctx.ps[2]], writes=["scC"], tag="hn2")
                eb.op("vector", lambda e, i: e.tensor_scalar(
                    scC, scC, lnw[:, 0:1], lnbc[:, 0:1], ALU.mult, ALU.add),
                    reads=["scC", lnw, lnbc], writes=["scC"], tag="hn3")
                eb.op("scalar", lambda e, i, m=m: e.activation(
                    scA, blk(gsh, m), AF.Sigmoid),
                    reads=[gsh], writes=["scA"], tag="sg1")
                eb.op("vector", lambda e, i, m=m: e.tensor_mul(
                    scA, scA, blk(gsh, m)),
                    reads=["scA", gsh], writes=["scA"], tag="sg2")
                eb.op("vector", lambda e, i, m=m: e.tensor_mul(
                    blk(osheet, m), scA, scC),
                    reads=["scA", "scC"], writes=["osh"], tag="ov")
                if dbm:
                    dout = ctx.outp("dYt", [128, TLRW], F32)
                    eb.dma("sync", dout, Yi, reads=["Yi"], tag="ddYt")
            if dbg and tc == 0:
                dout = ctx.outp("dS", [128, 512], F32)
                eb.dma("sync", dout, Ssheet[:], reads=["Ss"], tag="ddS")

            if stage >= 3:
                for mo in range(KT):
                    ps = ctx.ps[2 + mo % 2]
                    for kd in range(KT):
                        eb.op("tensor", lambda e, i, mo=mo, kd=kd, ps=ps:
                              e.matmul(ps[:], wslice(wO, D, kd, mo * 128),
                                       blk(osheet, kd),
                                       start=(kd == 0), stop=(kd == KT - 1)),
                              reads=[wO, "osh"], writes=[ps], tag="ow")
                    eb.dma("sync", xres,
                           x.rearrange("(k p) t -> p k t", p=128)
                           [:, mo, t0:t0 + TLRW],
                           writes=["xres"], tag="xrl")
                    eb.op("vector", lambda e, i, ps=ps: e.tensor_add(
                        xres, xres, ps[:]),
                        reads=["xres", ps], writes=["xres"], tag="ores")
                    eb.dma("sync",
                           xo.rearrange("(k p) t -> p k t", p=128)
                           [:, mo, t0:t0 + TLRW], xres,
                           reads=["xres"], tag="os")
            eb.full_barrier(tag=f"fb_b{tc}")
        eb.barrier("sync")
        eb.finish()
    return ctx.nc



def build_fused():
    with Ctx("mk") as ctx:
        eb = ctx.eb
        x0 = ctx.inp("x", [D, T])
        xoF = ctx.outp("xo", [D, T])
        chA = ctx.internal("chA", [D, T])
        chB = ctx.internal("chB", [D, T])
        Ctx.shared = ctx
        try:
            def st(pf, io, fn, **kw):
                Ctx.shared_prefix = pf
                Ctx.shared_io = io
                fn(**kw)
                Ctx.shared_io = None

            st("a_", {"x": x0, "xo": chA}, build_rwkv)
            for hf, pf in ((0, "b_"), (1024, "c_")):
                st(pf, {"x": chA[:, hf:hf + 1024],
                        "xo": chB[:, hf:hf + 1024]},
                   build_moe, feat_io=True)
            st("d_", {"x": chB, "xo": chA}, build_l0ab)
            for hf, pf in ((0, "e_"), (1024, "f_")):
                st(pf, {"x": chA[:, hf:hf + 1024],
                        "xo": chB[:, hf:hf + 1024]},
                   build_moe, feat_io=True)
            st("g_", {"x": chB, "xo": chA}, build_gru)
            st("h_", {"x": chA, "xo": chB}, build_l1ffn)
            st("i_", {"x": chB, "xo": chA}, build_l1ab)
            st("j_", {"x": chA, "xo": xoF}, build_l1ffn)
        finally:
            Ctx.shared = None
            Ctx.shared_io = None
            Ctx.shared_prefix = ""
        eb.defer = False
        eb.barrier("sync")
        eb.finish()
    return ctx.nc
import numpy as np

from concourse.bass_utils import run_bass_kernel_spmd as _run_spmd


def _col8(v):
    return np.asarray(v, np.float32).reshape(8, 128).T.copy()


def host_prep_rwkv(x_T, xw, Ws, loras, ln_w, ln_b, oW, init_state, u):
    ins = dict(x=np.ascontiguousarray(x_T, np.float32))
    for nm, W in zip(("rwT", "kwT", "vwT", "gwT", "owT"),
                     (Ws["r"], Ws["k"], Ws["v"], Ws["g"], oW)):
        ins[nm] = np.ascontiguousarray(np.asarray(W, np.float32).T)
    for p in "rkvgd":
        la, lb, ll = loras[p]
        ins[f"la{p}"] = np.ascontiguousarray(np.asarray(la, np.float32).T)
        ins[f"lb{p}"] = np.ascontiguousarray(np.asarray(lb, np.float32).T)
        ins[f"ll{p}"] = _col8(ll)
    ins["xwc"] = _col8(xw)
    ins["uc"] = _col8(u)
    ins["lnw"] = np.tile(np.asarray(ln_w, np.float32), 2)[:, None].copy()
    ins["lnbc"] = np.tile(np.asarray(ln_b, np.float32), 2)[:, None].copy()
    sinit = np.zeros((128, 512), np.float32)
    ist = np.asarray(init_state, np.float32)
    for m in range(8):
        for h2 in range(2):
            sinit[h2 * 64:(h2 + 1) * 64, m * 64:(m + 1) * 64] = ist[2 * m + h2]
    ins["sinit"] = sinit
    return ins


def host_prep_moe(xh_TM, gate_w, bias, up_w, up_b, down_w, down_b,
                  sw_up, sb_up, sw_down, sb_down):
    upb = np.zeros((128, 8 * 32), np.float32)
    for e in range(8):
        upb[:, e * 32:(e + 1) * 32] = np.asarray(
            up_b[e], np.float32).reshape(32, 128).T
    dbal = np.zeros((2, 128, 1024), np.float32)
    for e in range(8):
        dbal[e // 4, 32 * (e % 4), :] = down_b[e]
    return dict(
        x=np.ascontiguousarray(xh_TM.T, np.float32),
        xTM=np.ascontiguousarray(xh_TM, np.float32),
        gate_w=np.ascontiguousarray(gate_w, np.float32),
        gbrow=np.asarray(bias, np.float32).reshape(1, 8).copy(),
        up_w=np.ascontiguousarray(up_w, np.float32),
        upb=upb,
        down_w=np.ascontiguousarray(down_w, np.float32),
        dbal=dbal,
        sw_up=np.ascontiguousarray(sw_up, np.float32),
        sbu=np.asarray(sb_up, np.float32).reshape(32, 128).T.copy(),
        sw_down=np.ascontiguousarray(sw_down, np.float32),
        sbdrow=np.asarray(sb_down, np.float32).reshape(1, 1024).copy(),
    )


def host_prep_l0ab(xb_T, wqkv, wo, b_o, gamma):
    g = np.asarray(gamma, np.float32).reshape(16)
    ghi = np.repeat(g, 64).reshape(8, 128).T.copy()
    g2 = np.stack([g[0::2], g[1::2]], 0).copy()
    return dict(
        x=np.ascontiguousarray(xb_T, np.float32),
        wqkv=np.ascontiguousarray(wqkv, np.float32),
        wo=np.ascontiguousarray(wo, np.float32),
        bo=np.asarray(b_o, np.float32).reshape(8, 128).T.copy(),
        ghi=np.ascontiguousarray(ghi, np.float32),
        g2=np.ascontiguousarray(g2, np.float32),
    )


def host_prep_gru(x_T, ln_g, ln_b, W_in, b_in, W_h, b_h, W_out, b_out):
    ins = dict(x=np.ascontiguousarray(x_T, np.float32))
    for l in range(2):
        ins[f"lng{l}"] = _col8(ln_g[l])
        ins[f"lnb{l}"] = _col8(ln_b[l])
        ins[f"winT{l}"] = np.ascontiguousarray(np.asarray(W_in[l], np.float32).T)
        ins[f"binu{l}"] = _col8(b_in[l][:1024])
        ins[f"binv{l}"] = _col8(b_in[l][1024:])
        ins[f"bh{l}"] = _col8(b_h[l])
        ins[f"whT{l}"] = np.ascontiguousarray(np.asarray(W_h[l], np.float32).T)
        ins[f"woutT{l}"] = np.ascontiguousarray(np.asarray(W_out[l], np.float32).T)
        ins[f"bout{l}"] = _col8(b_out[l])
    return ins


def host_prep_l1ffn(x_T, wqkv, q_scale, k_scale, wo, ln_g, ln_b, w1, b1, w2, b2):
    ins = dict(x=np.ascontiguousarray(x_T, np.float32))
    for l in range(2):
        ins[f"wqkv{l}"] = np.ascontiguousarray(wqkv[l], np.float32)
        qs = np.asarray(q_scale[l], np.float32)
        ks = np.asarray(k_scale[l], np.float32)
        ins[f"qsc{l}"] = np.tile(qs, 2)[:, None].copy()
        ins[f"ksc{l}"] = np.tile(ks, 2)[:, None].copy()
        ins[f"qss{l}"] = np.tile(np.roll(qs, -32), 2)[:, None].copy()
        ins[f"kss{l}"] = np.tile(np.roll(ks, -32), 2)[:, None].copy()
        ins[f"lg{l}"] = _col8(ln_g[l])
        ins[f"lb{l}"] = _col8(ln_b[l])
        w1p = np.zeros((1024, 5632), np.float32)
        w1p[:, 0:2730] = w1[l][:, 0:2730]
        w1p[:, 2816:2816 + 2730] = w1[l][:, 2730:5460]
        ins[f"w1p{l}"] = w1p
        b1p = np.zeros(2816, np.float32)
        b1p[0:2730] = b1[l][0:2730]
        ins[f"b1a{l}"] = b1p.reshape(22, 128).T.copy()
        b1g = np.zeros(2816, np.float32)
        b1g[0:2730] = b1[l][2730:5460]
        ins[f"b1g{l}"] = b1g.reshape(22, 128).T.copy()
        w2p = np.zeros((2816, 1024), np.float32)
        w2p[0:2730] = w2[l]
        ins[f"w2p{l}"] = w2p
        ins[f"b2{l}"] = _col8(b2[l])
        ins[f"wo{l}"] = np.ascontiguousarray(wo[l], np.float32)
    return ins


def host_prep_l1ab(x_T, q_a, q_b, k_a, k_b, v_w, v_b, o_w, o_b):
    H = 16
    qbs = np.zeros((64, H * 64), np.float32)
    kbs = np.zeros((64, H * 64), np.float32)
    for h in range(H):
        qbs[:, h * 64:(h + 1) * 64] = q_b[h]
        kbs[:, h * 64:(h + 1) * 64] = k_b[h]
    return dict(
        x=np.ascontiguousarray(x_T, np.float32),
        qa=np.ascontiguousarray(q_a, np.float32),
        ka=np.ascontiguousarray(k_a, np.float32),
        qbs=qbs, kbs=kbs,
        vwT2=np.ascontiguousarray(np.asarray(v_w, np.float32).T),
        vb=_col8(v_b),
        owT2=np.ascontiguousarray(np.asarray(o_w, np.float32).T),
        ob=_col8(o_b),
    )


_PROGS = {}


def _prog(name, builder):
    if name not in _PROGS:
        _PROGS[name] = builder()
    return _PROGS[name]


def kernel(**inputs):
    B = 8
    g = {k: np.asarray(v) for k, v in inputs.items()}
    x = np.asarray(g["L0_attn_a_x"], np.float32)
    cores = list(range(B))

    Ws = dict(r=g["L0_attn_a_rW"], k=g["L0_attn_a_kW"],
              v=g["L0_attn_a_vW"], g=g["L0_attn_a_gW"])
    loras = {p: (g[f"L0_attn_a_{p}_la"], g[f"L0_attn_a_{p}_lb"],
                 g[f"L0_attn_a_{p}_ll"]) for p in "rkvgd"}
    dummy = np.zeros((1024, 1024), np.float32)
    wmoe = host_prep_moe(dummy, g["L0_ffn_gate_w"], g["L0_ffn_bias"],
                         g["L0_ffn_up_w"], g["L0_ffn_up_b"],
                         g["L0_ffn_down_w"], g["L0_ffn_down_b"],
                         g["L0_ffn_sw_up"], g["L0_ffn_sb_up"],
                         g["L0_ffn_sw_down"], g["L0_ffn_sb_down"])
    wmoe.pop("x"); wmoe.pop("xTM")
    w0ab = host_prep_l0ab(dummy[:, :1], g["L0_attn_b_W_qkv"],
                          g["L0_attn_b_W_o"], g["L0_attn_b_b_o"],
                          g["L0_attn_b_gamma"])
    w0ab.pop("x")
    wgru = host_prep_gru(dummy[:, :1], g["L1_attn_a_ln_g"],
                         g["L1_attn_a_ln_b"], g["L1_attn_a_W_in"],
                         g["L1_attn_a_b_in"], g["L1_attn_a_W_h"],
                         g["L1_attn_a_b_h"], g["L1_attn_a_W_out"],
                         g["L1_attn_a_b_out"])
    wgru.pop("x")
    wffn = host_prep_l1ffn(dummy[:, :1], g["L1_ffn_wqkv"],
                           g["L1_ffn_q_scale"], g["L1_ffn_k_scale"],
                           g["L1_ffn_wo"], g["L1_ffn_ln_g"],
                           g["L1_ffn_ln_b"], g["L1_ffn_w1"], g["L1_ffn_b1"],
                           g["L1_ffn_w2"], g["L1_ffn_b2"])
    wffn.pop("x")
    w1ab = host_prep_l1ab(dummy[:, :1], g["L1_attn_b_q_a"],
                          g["L1_attn_b_q_b"], g["L1_attn_b_k_a"],
                          g["L1_attn_b_k_b"], g["L1_attn_b_v_w"],
                          g["L1_attn_b_v_b"], g["L1_attn_b_o_w"],
                          g["L1_attn_b_o_b"])
    w1ab.pop("x")

    ins_list = []
    for b in range(B):
        ins = host_prep_rwkv(np.ascontiguousarray(x[b].T),
                             g["L0_attn_a_xw"], Ws, loras,
                             g["L0_attn_a_ln_w"], g["L0_attn_a_ln_b"],
                             g["L0_attn_a_oW"], g["L0_attn_a_init_state"],
                             g["L0_attn_a_u"])
        ins.update(wmoe)
        ins.update(w0ab)
        ins.update(wgru)
        ins.update(wffn)
        ins.update(w1ab)
        ins_list.append(ins)

    nc = _prog("fused", build_fused)
    res = _run_spmd(nc, ins_list, core_ids=cores)
    return np.stack([np.array(res.results[b]["xo"], np.float32).T
                     for b in range(B)])
